# version 05
# baseline (speedup 1.0000x reference)
import numpy as np
import ml_dtypes
from contextlib import ExitStack

import concourse.bass as bass
import concourse.mybir as mybir
from concourse.bass_utils import run_bass_kernel_spmd
from concourse.bass import IndirectOffsetOnAxis

F32 = mybir.dt.float32
BF16 = mybir.dt.bfloat16
I32 = mybir.dt.int32
U8 = mybir.dt.uint8
AF = mybir.ActivationFunctionType
ALU = mybir.AluOpType
AX = mybir.AxisListType
BF = ml_dtypes.bfloat16

D = 1024
NCHUNK = D // 128
H = 16
HD = 64
DL = 256
LAM = 0.01
NB = 4
NCORES = 8
EPS = 1e-5
DFF = 4096
RANK = 256
NEXP = 8

ENGS = ("sync", "vector", "scalar", "tensor", "gpsimd")


class Prog:
    def __init__(self, name, scratch=16384):
        self.name = name
        self.nc = bass.Bass("TRN2", target_bir_lowering=False, debug=False,
                            dynamic_dma_scratch_size=scratch)
        self.es = ExitStack()
        self.sems = {}
        self.cnt = {}
        self.block = None
        self._inputs = {}
        self._outputs = {}

    def inp(self, name, shape, dt=F32):
        ap = self.nc.dram_tensor(name, list(shape), dt, kind="ExternalInput").ap()
        self._inputs[name] = ap
        return ap

    def out(self, name, shape, dt=F32):
        ap = self.nc.dram_tensor(name, list(shape), dt, kind="ExternalOutput").ap()
        self._outputs[name] = ap
        return ap

    def dram(self, name, shape, dt=F32):
        return self.nc.dram_tensor(name, list(shape), dt).ap()

    def sb(self, name, p, f, dt=F32):
        return self.es.enter_context(self.nc.sbuf_tensor(name, [p, f], dt))

    def ps(self, name, p, f, dt=F32):
        return self.es.enter_context(self.nc.psum_tensor(name, [p, f], dt))

    def sem(self, name):
        h = self.es.enter_context(self.nc.semaphore(name))
        self.sems[name] = h
        self.cnt[name] = 0
        return h

    def begin(self):
        self.block = self.es.enter_context(self.nc.Block())

    def finish(self):
        self.es.close()
        return self.nc

    def inc(self, instr, sem, k=1):
        instr.then_inc(self.sems[sem], k)
        self.cnt[sem] += k
        return self.cnt[sem]

    def tok(self, sem):
        return self.cnt[sem]

    def wait(self, eng, sem, val=None):
        eng.wait_ge(self.sems[sem], self.cnt[sem] if val is None else val)

    def on(self, engname):
        return getattr(self.block, engname)


def ceil_div(a, b):
    return (a + b - 1) // b


def queue_load_weight(p, hbm_ap, rows, cols, name, ld_sem):
    nchunk = rows // 128
    s32 = p.sb(name + "_f32", 128, nchunk * cols, F32)

    @p.on("sync")
    def _(sync):
        dma = sync.dma_start(out=s32[:], in_=hbm_ap.rearrange("(c p) n -> p c n", p=128))
        p.inc(dma, ld_sem, 16)
    return s32


def cast_weight(p, s32, name, ld_sem, cast_sem, ld_total):
    sbf = p.sb(name + "_bf", s32.shape[0], s32.shape[1], BF16)

    @p.on("vector")
    def _(v):
        p.wait(v, ld_sem, ld_total)
        p.inc(v.tensor_copy(sbf[:], s32[:]), cast_sem)
    return sbf


def emit_prologue(p, xprev, p1, p2, mode, T, xcur_hbm, xln_hbm, xfm, eps_col, n, nbuf=2):
    ntile = T // 128
    xt = [p.sb(f"{n}_xt{i}", 128, D, F32) for i in range(nbuf)]
    pt1 = [p.sb(f"{n}_p1{i}", 128, D, F32) for i in range(nbuf)]
    pt2 = [p.sb(f"{n}_p2{i}", 128, D, F32) for i in range(nbuf)]
    xc = xt
    rb = [p.sb(f"{n}_rb{i}", 128, D, BF16) for i in range(nbuf)]
    xl = [p.sb(f"{n}_xl{i}", 128, D, BF16) for i in range(nbuf)]
    st12 = [p.sb(f"{n}_st{i}", 128, 12, F32) for i in range(nbuf)]
    mv = [p.sb(f"{n}_mv{i}", 128, 2, F32) for i in range(nbuf)]
    rstd = [p.sb(f"{n}_rs{i}", 128, 1, F32) for i in range(nbuf)]
    for bb in range(2):
        p.sem(f"{n}_l{bb}")
        p.sem(f"{n}_o{bb}")
        p.sem(f"{n}_c{bb}")
    p.sem(n + "_v")
    p.sem(n + "_s")
    p.sem(n + "_a")
    p.sem(n + "_f")

    for i in range(ntile):
        b = i % nbuf
        r = slice(i * 128, (i + 1) * 128)

        @p.on("sync")
        def _(sync, i=i, b=b, r=r):
            if i >= nbuf:
                p.wait(sync, n + "_a", i - (nbuf - 1))
                p.wait(sync, f"{n}_c{b}", 16 * (i // nbuf))
            dma = sync.dma_start(out=xt[b][:], in_=xprev[r, :])
            p.inc(dma, f"{n}_l{b}", 16)
            dma = sync.dma_start(out=pt1[b][:], in_=p1[r, :])
            p.inc(dma, f"{n}_l{b}", 16)
            dma = sync.dma_start(out=pt2[b][:], in_=p2[r, :])
            p.inc(dma, f"{n}_l{b}", 16)

        @p.on("vector")
        def _(v, i=i, b=b):
            p.wait(v, f"{n}_l{b}", 48 * (i // nbuf + 1))
            if i >= nbuf:
                p.wait(v, n + "_s", 2 * (i - (nbuf - 1)))
            if mode == "add":
                p.inc(v.tensor_add(xc[b][:], xt[b][:], pt1[b][:]), n + "_v")
                p.wait(v, n + "_v")
                p.inc(v.tensor_add(xc[b][:], xc[b][:], pt2[b][:]), n + "_v")
            else:
                p.inc(v.tensor_add(pt1[b][:], pt1[b][:], pt2[b][:]), n + "_v")
                p.wait(v, n + "_v")
                p.inc(v.tensor_copy(rb[b][:], pt1[b][:]), n + "_v")
                p.wait(v, n + "_v")
                p.inc(v.tensor_add(xc[b][:], xt[b][:], rb[b][:]), n + "_v")
            p.wait(v, n + "_v")
            p.inc(v.bn_stats(st12[b][:, 0:6], xc[b][:, 0:512]), n + "_v")
            p.inc(v.bn_stats(st12[b][:, 6:12], xc[b][:, 512:1024]), n + "_v")
            p.wait(v, n + "_v")
            p.inc(v.bn_aggr(mv[b][:], st12[b][:]), n + "_v")

        @p.on("scalar")
        def _(s, i=i, b=b):
            p.wait(s, n + "_v")
            if i >= nbuf:
                p.wait(s, n + "_a", i - (nbuf - 1))
            p.inc(s.activation(rstd[b][:], mv[b][:, 1:2], AF.Ln, bias=eps_col), n + "_s")
            p.wait(s, n + "_s")
            p.inc(s.activation(rstd[b][:], rstd[b][:], AF.Exp, scale=-0.5), n + "_s")

        @p.on("vector")
        def _(v, i=i, b=b):
            p.wait(v, n + "_s", 2 * (i + 1))
            if i >= nbuf:
                p.wait(v, f"{n}_o{b}", 16 * (i // nbuf))
            p.inc(v.tensor_scalar(xl[b][:], xc[b][:], mv[b][:, 0:1], rstd[b][:, 0:1],
                                  ALU.subtract, ALU.mult), n + "_a")

        @p.on("sync")
        def _(sync, i=i, b=b, r=r):
            p.wait(sync, n + "_a", i + 1)
            dma = sync.dma_start(out=xln_hbm[r, :], in_=xl[b][:])
            p.inc(dma, f"{n}_o{b}", 16)
            dma = sync.dma_start(out=xcur_hbm[r, :], in_=xc[b][:])
            p.inc(dma, f"{n}_c{b}", 16)

    @p.on("scalar")
    def _(s):
        p.wait(s, n + "_o0")
        p.wait(s, n + "_o1")
        for c in range(NCHUNK):
            dma = s.dma_start_transpose(out=xfm[c][:], in_=xln_hbm[:, c * 128:(c + 1) * 128])
            p.inc(dma, n + "_f", 16)


def build_l1ffn(T, mode="add", dbg=False):
    p = Prog("l1ffn")
    xprev = p.inp("xprev", [T, D])
    part1 = p.inp("part1", [T, D])
    part2 = p.inp("part2", [T, D])
    wU1 = p.inp("U1", [D, RANK])
    wV1 = p.inp("V1", [RANK, DFF])
    wb1 = p.inp("b1c", [128, DFF // 128])
    wU2 = p.inp("U2", [DFF, RANK])
    wV2 = p.inp("V2", [RANK, D])
    wb2 = p.inp("b2r", [1, D])
    cs = p.inp("cscal", [128, 4])
    y = p.out("xn", [T, D])
    mk = p.out if dbg else p.dram
    xln_hbm = mk("xlnh", [T, D], BF16)
    xcur = mk("xcur", [T, D], F32)

    ntc = T // 512

    xfm = [p.sb(f"xfm{c}", 128, T, BF16) for c in range(NCHUNK)]
    csb = p.sb("csb", 128, 4, F32)
    b2r = p.sb("b2rs", 1, D, F32)
    b2rb = p.sb("b2rb", 1, D, BF16)
    onesr = p.sb("onesr", 1, 128, BF16)
    b1c = p.sb("b1cs", 128, DFF // 128, F32)
    Pfm = [p.sb(f"Pfm{rc}", 128, T, BF16) for rc in range(2)]
    Hbf = [p.sb(f"Hbf{i}", 128, 512, BF16) for i in range(2)]
    Qbf = [p.sb(f"Qbf{rc}", 128, 512, BF16) for rc in range(2)]
    if T >= 4096:
        xres = [xfm[2][:].bitcast(F32)[:, i * D:(i + 1) * D] for i in range(2)]
        yt = [xfm[3][:].bitcast(F32)[:, i * D:(i + 1) * D] for i in range(2)]
    else:
        xres = [p.sb(f"xres{i}", 128, D, F32)[:] for i in range(2)]
        yt = [p.sb(f"yt{i}", 128, D, F32)[:] for i in range(2)]

    ps_p = p.ps("ps_p", 128, 512)
    ps_h = [p.ps(f"ps_h{i}", 128, 512) for i in range(2)]
    ps_q = [p.ps(f"ps_q{rc}", 128, 512) for rc in range(2)]
    ps_o = [p.ps(f"ps_o{i}", 128, 512) for i in range(2)]

    for sname in ("ld", "cast", "mm", "ev", "ge", "qmm",
                  "r0", "r1", "y0", "y1"):
        p.sem(sname)
    p.begin()

    @p.on("sync")
    def _(sync):
        p.inc(sync.dma_start(out=csb[:], in_=cs), "ld", 16)
        p.inc(sync.dma_start(out=b2r[:], in_=wb2), "ld", 16)
        p.inc(sync.dma_start(out=b1c[:], in_=wb1), "ld", 16)
    U1b = p.sb("U1b", 128, NCHUNK * RANK, BF16)
    V1b = p.sb("V1b", 128, 2 * DFF, BF16)
    U2b = p.sb("U2b", 128, 32 * RANK, BF16)
    V2b = p.sb("V2b", 128, 2 * D, BF16)
    wsems = [emit_weight_to(p, wU1, D, RANK, U1b, "U1", None, "cast"),
             emit_weight_to(p, wV1, RANK, DFF, V1b, "V1", None, "cast"),
             emit_weight_to(p, wU2, DFF, RANK, U2b, "U2", None, "cast"),
             emit_weight_to(p, wV2, RANK, D, V2b, "V2", None, "cast")]
    ld_total = p.tok("ld")

    @p.on("vector")
    def _(v):
        p.wait(v, "ld", ld_total)
        p.inc(v.memset(onesr[:], 1.0), "cast")
        p.inc(v.tensor_copy(b2rb[:], b2r[:]), "cast")
    wcast = p.tok("cast")

    @p.on("scalar")
    def _(s):
        p.wait(s, "ld", ld_total)
    emit_prologue(p, xprev, part1, part2, mode, T, xcur, xln_hbm, xfm, csb[:, 0:1], "ln")
    fmsem = "ln_f"
    fm_tok = p.tok(fmsem)

    first = True
    for rc in range(2):
        for tc in range(ntc):
            @p.on("tensor")
            def _(t, rc=rc, tc=tc, first=first):
                if first:
                    p.wait(t, fmsem, fm_tok)
                    p.wait(t, "cast", wcast)
                    for ws in wsems:
                        p.wait(t, ws)
                else:
                    p.wait(t, "ev", p.tok("ev"))
                for dc in range(NCHUNK):
                    ins = t.matmul(ps_p[:],
                                   U1b[:, dc * RANK + rc * 128: dc * RANK + rc * 128 + 128],
                                   xfm[dc][:, tc * 512:(tc + 1) * 512],
                                   start=(dc == 0), stop=(dc == NCHUNK - 1))
                p.inc(ins, "mm")

            @p.on("vector")
            def _(v, rc=rc, tc=tc):
                p.wait(v, "mm", p.tok("mm"))
                p.inc(v.tensor_copy(Pfm[rc][:, tc * 512:(tc + 1) * 512], ps_p[:]), "ev")
            first = False
    p_done = p.tok("ev")

    for tc in range(ntc):
        for fs in range(32):
            hb = fs % 2

            @p.on("tensor")
            def _(t, tc=tc, fs=fs, hb=hb):
                if tc == 0 and fs == 0:
                    p.wait(t, "ev", p.tok("ev"))
                if fs == 0 and tc > 0:
                    p.wait(t, "ev", p.tok("ev"))
                if fs >= 2:
                    p.wait(t, "ge", p.tok("ge"))
                for rc in range(2):
                    ins = t.matmul(ps_h[hb][:],
                                   V1b[:, rc * DFF + fs * 128: rc * DFF + fs * 128 + 128],
                                   Pfm[rc][:, tc * 512:(tc + 1) * 512],
                                   start=(rc == 0), stop=(rc == 1))
                p.inc(ins, "mm")

            @p.on("scalar")
            def _(s, fs=fs, hb=hb):
                p.wait(s, "mm", p.tok("mm"))
                p.wait(s, "qmm", p.tok("qmm"))
                p.inc(s.activation(Hbf[hb][:], ps_h[hb][:], AF.Gelu,
                                   bias=b1c[:, fs:fs + 1]), "ge")

            @p.on("tensor")
            def _(t, fs=fs, hb=hb):
                p.wait(t, "ge", p.tok("ge"))
                for rc in range(2):
                    ins = t.matmul(ps_q[rc][:],
                                   U2b[:, fs * RANK + rc * 128: fs * RANK + rc * 128 + 128],
                                   Hbf[hb][:],
                                   start=(fs == 0), stop=(fs == 31))
                p.inc(ins, "qmm")

        @p.on("vector")
        def _(v):
            p.wait(v, "qmm", p.tok("qmm"))
            p.inc(v.tensor_copy(Qbf[0][:], ps_q[0][:]), "ev")
            p.inc(v.tensor_copy(Qbf[1][:], ps_q[1][:]), "ev")
        q_ev = p.tok("ev")

        for tsub in range(4):
            tt = tc * 4 + tsub
            ob = tsub % 2
            rs = f"r{ob}"
            ys = f"y{ob}"

            @p.on("sync")
            def _(sync, tt=tt, ob=ob, rs=rs, ys=ys):
                if tt >= 2:
                    p.wait(sync, ys, p.tok(ys))
                if tt < 2:
                    p.wait(sync, "ev", p_done)
                if tt == 0:
                    p.wait(sync, "ln_c0")
                    p.wait(sync, "ln_c1")
                dma = sync.dma_start(out=xres[ob], in_=xcur[tt * 128:(tt + 1) * 128, :])
                p.inc(dma, rs, 16)
            res_tok = p.tok(rs)

            for dc2 in range(2):
                @p.on("tensor")
                def _(t, tsub=tsub, ob=ob, dc2=dc2, q_ev=q_ev):
                    p.wait(t, "ev", q_ev)
                    p.wait(t, "ev", p.tok("ev"))
                    for rc in range(2):
                        t.matmul(ps_o[ob][:],
                                 Qbf[rc][:, tsub * 128:(tsub + 1) * 128],
                                 V2b[:, rc * D + dc2 * 512: rc * D + dc2 * 512 + 512],
                                 start=(rc == 0), stop=False)
                    ins = t.matmul(ps_o[ob][:], onesr[:],
                                   b2rb[:, dc2 * 512:(dc2 + 1) * 512],
                                   start=False, stop=True)
                    p.inc(ins, "mm")

                @p.on("vector")
                def _(v, ob=ob, dc2=dc2, rs=rs, res_tok=res_tok):
                    p.wait(v, "mm", p.tok("mm"))
                    p.wait(v, rs, res_tok)
                    p.inc(v.tensor_add(yt[ob][:, dc2 * 512:(dc2 + 1) * 512],
                                       xres[ob][:, dc2 * 512:(dc2 + 1) * 512],
                                       ps_o[ob][:]), "ev")

            @p.on("sync")
            def _(sync, tt=tt, ob=ob, ys=ys):
                p.wait(sync, "ev", p.tok("ev"))
                dma = sync.dma_start(out=y[tt * 128:(tt + 1) * 128, :], in_=yt[ob])
                p.inc(dma, ys, 16)

    @p.on("sync")
    def _(sync):
        p.wait(sync, "y0")
        p.wait(sync, "y1")

    p.finish()
    return p


def host_l1ffn_weights(inp):
    return dict(
        U1=np.ascontiguousarray(inp["L1_ffn_U1"]),
        V1=np.ascontiguousarray(inp["L1_ffn_V1"]),
        b1c=np.ascontiguousarray(inp["L1_ffn_b1"].reshape(DFF // 128, 128).T),
        U2=np.ascontiguousarray(inp["L1_ffn_U2"]),
        V2=np.ascontiguousarray(inp["L1_ffn_V2"]),
        b2r=np.ascontiguousarray(inp["L1_ffn_b2"].reshape(1, D)),
        cscal=np.tile(np.array([[EPS, 1.0, -6.0, -4.0]], np.float32), (128, 1)),
    )

def emit_weight_to(p, hbm_ap, rows, cols, dst_bf, name, sp_unused, cast_sem, deps=()):
    wsem = "w_" + name
    p.sem(wsem)

    @p.on("gpsimd")
    def _(g):
        for sm, tk in deps:
            p.wait(g, sm, tk)
        dma = g.dma_start(out=dst_bf[:], in_=hbm_ap.rearrange("(c p) n -> p c n", p=128))
        p.inc(dma, wsem, 16)
    return wsem


class StagePair:
    def __init__(self, p, name, cols):
        pass


def build_l0ab(T, mode="round"):
    p = Prog("l0ab")
    xprev = p.inp("xprev", [T, D])
    part1 = p.inp("part1", [T, D])
    part2 = p.inp("part2", [T, D])
    wQ = p.inp("WqT", [D, D])
    wV = p.inp("WvT", [D, D])
    wG = p.inp("WgT", [D, D])
    wO = p.inp("WoT", [D, D])
    beta = p.inp("beta", [16, 1])
    msel = p.inp("Msel", [16, D])
    gam = p.inp("gamc", [128, NCHUNK])
    gbe = p.inp("gbec", [128, NCHUNK])
    cs = p.inp("cscal", [128, 4])
    xn = p.out("xn", [T, D])
    xcur = p.dram("xcur", [T, D], F32)
    xlnh = p.dram("xlnh", [T, D], BF16)
    ybh = p.dram("ybh", [D, T], BF16)

    ntt = T // 128
    ntc = T // 512

    xfm = [p.sb(f"xfm{c}", 128, T, BF16) for c in range(NCHUNK)]
    csb = p.sb("csb", 128, 4, F32)
    WA = p.sb("WA", 128, 8 * D, BF16)
    WB = p.sb("WB", 128, 8 * D, BF16)

    bet = p.sb("bet", 16, 1, F32)
    lam16 = p.sb("lam16", 16, 1, F32)
    msl = p.sb("msl", 16, D, F32)
    lamc = p.sb("lamc", 128, NCHUNK, F32)
    gams = p.sb("gams", 128, NCHUNK, F32)
    gbes = p.sb("gbes", 128, NCHUNK, F32)
    lamt = p.sb("lamt", 128, T, BF16)
    vch = p.sb("vch", 128, T, BF16)
    state = p.sb("state", 128, T, F32)
    ytl = vch
    yld = [p.sb(f"yld{i}", 128, 512, BF16) for i in range(2)]
    y2t = p.sb("y2t", 128, 512, BF16)
    srow = p.sb("srow", 1, T, F32)
    sqrow = p.sb("sqrow", 1, T, F32)
    murow = srow
    rsrow = sqrow
    ones1 = p.sb("ones1", 128, 1, BF16)
    onesw = p.sb("onesw", 1, 128, F32)
    if T >= 2048:
        ygt = ([vch[:, c * 512:(c + 1) * 512] for c in range(4)]
               + [lamt[:, c * 512:(c + 1) * 512] for c in range(4)])
    else:
        ygt = [p.sb(f"ygt{c}", 128, 512, BF16)[:] for c in range(NCHUNK)]
    gbf = p.sb("gbf", 128, 512, BF16)
    t1b = p.sb("t1b", 128, 512, F32)
    if T >= 4 * D:
        xrt = [state[:, i * D:(i + 1) * D] for i in range(2)]
        xot = [state[:, (2 + i) * D:(3 + i) * D] for i in range(2)]
    else:
        xrt = [p.sb(f"xrt{i}", 128, D, F32)[:] for i in range(2)]
        xot = [p.sb(f"xot{i}", 128, D, F32)[:] for i in range(2)]

    pool = [p.ps(f"pp{i}", 128, 512) for i in range(5)]
    ps_v = pool[0]
    ps_q = pool[1]
    ps_lam = pool[2]
    ps_s1 = pool[0]
    ps_s2 = pool[1]
    ps_b1 = pool[0]
    ps_b2 = pool[1]
    ps_g = pool[2]
    ps_o = [pool[3], pool[4]]

    for sname in ("ld", "cast", "cast2", "mm", "ev", "sc", "scan",
                  "st0", "st1", "yl0", "yl1", "smm", "sev", "gmm", "gact",
                  "omm", "oev", "r0", "r1", "x0", "x1"):
        p.sem(sname)
    p.begin()

    @p.on("sync")
    def _(sync):
        p.inc(sync.dma_start(out=csb[:], in_=cs), "ld", 16)
        p.inc(sync.dma_start(out=bet[:], in_=beta), "ld", 16)
        p.inc(sync.dma_start(out=msl[:], in_=msel), "ld", 16)
        p.inc(sync.dma_start(out=gams[:], in_=gam), "ld", 16)
        p.inc(sync.dma_start(out=gbes[:], in_=gbe), "ld", 16)
    ld0 = p.tok("ld")

    @p.on("scalar")
    def _(s):
        p.wait(s, "ld", ld0)
        p.inc(s.activation(lam16[:], bet[:], AF.Sigmoid), "sc")

    @p.on("vector")
    def _(v):
        p.wait(v, "ld", ld0)
        p.inc(v.memset(ones1[:], 1.0), "cast")
        p.inc(v.memset(onesw[:], 1.0), "cast")

    for c in range(NCHUNK):
        @p.on("tensor")
        def _(t, c=c):
            p.wait(t, "sc", 1)
            p.wait(t, "ev", p.tok("ev"))
            ins = t.matmul(ps_lam[:, 0:1], msl[:, c * 128:(c + 1) * 128], lam16[:],
                           start=True, stop=True)
            p.inc(ins, "mm")

        @p.on("vector")
        def _(v, c=c):
            p.wait(v, "mm")
            p.inc(v.tensor_copy(lamc[:, c:c + 1], ps_lam[:, 0:1]), "ev")
    lam_ev = p.tok("ev")

    sp = StagePair(p, "stg", D)
    wq_tok = emit_weight_to(p, wQ, D, D, WA, "Wq", sp, "cast")
    wv_tok = emit_weight_to(p, wV, D, D, WB, "Wv", sp, "cast")

    @p.on("scalar")
    def _(s):
        p.wait(s, "ld", ld0)
    emit_prologue(p, xprev, part1, part2, mode, T, xcur, xlnh, xfm, csb[:, 0:1], "ln", nbuf=1)
    fm_tok = p.tok("ln_f")

    for dc in range(NCHUNK):
        for tc in range(ntc):
            @p.on("tensor")
            def _(t, dc=dc, tc=tc):
                if dc == 0 and tc == 0:
                    p.wait(t, "ln_f", fm_tok)
                    p.wait(t, wv_tok)
                if tc == 0 and dc > 0:
                    pass
                p.wait(t, "ev", p.tok("ev"))
                for k in range(NCHUNK):
                    ins = t.matmul(ps_v[:], WB[:, k * D + dc * 128: k * D + dc * 128 + 128],
                                   xfm[k][:, tc * 512:(tc + 1) * 512],
                                   start=(k == 0), stop=(k == NCHUNK - 1))
                p.inc(ins, "mm")

            @p.on("vector")
            def _(v, dc=dc, tc=tc):
                p.wait(v, "mm", p.tok("mm"))
                if dc > 0 and tc == 0:
                    p.wait(v, "scan", p.tok("scan"))
                p.inc(v.tensor_copy(vch[:, tc * 512:(tc + 1) * 512], ps_v[:]), "ev")

        @p.on("vector")
        def _(v, dc=dc):
            if dc > 0:
                p.wait(v, "scan", p.tok("scan"))
            p.inc(v.memset(lamt[:], 1.0), "ev")
            p.wait(v, "ev")
            p.wait(v, "ev", lam_ev)
            p.inc(v.tensor_scalar(lamt[:], lamt[:], lamc[:, dc:dc + 1], None,
                                  ALU.mult, ALU.bypass), "ev")

        @p.on("vector")
        def _(v, dc=dc):
            p.wait(v, "ev", p.tok("ev"))
            p.inc(v.tensor_tensor_scan(state[:], lamt[:], vch[:], 0.0,
                                       ALU.mult, ALU.add), "scan")

        for tc in range(ntc):
            @p.on("tensor")
            def _(t, dc=dc, tc=tc):
                if dc == 0 and tc == 0:
                    p.wait(t, wq_tok)
                p.wait(t, "ev", p.tok("ev"))
                for k in range(NCHUNK):
                    ins = t.matmul(ps_q[:], WA[:, k * D + dc * 128: k * D + dc * 128 + 128],
                                   xfm[k][:, tc * 512:(tc + 1) * 512],
                                   start=(k == 0), stop=(k == NCHUNK - 1))
                p.inc(ins, "mm")

            @p.on("vector")
            def _(v, dc=dc, tc=tc):
                p.wait(v, "mm", p.tok("mm"))
                p.wait(v, "scan", p.tok("scan"))
                p.inc(v.tensor_mul(ytl[:, tc * 512:(tc + 1) * 512], ps_q[:],
                                   state[:, tc * 512:(tc + 1) * 512]), "ev")

        @p.on("sync")
        def _(sync, dc=dc):
            p.wait(sync, "ev", p.tok("ev"))
            dma = sync.dma_start(out=ybh[dc * 128:(dc + 1) * 128, :], in_=ytl[:])
            p.inc(dma, "yl0", 16)

        @p.on("vector")
        def _(v, dc=dc):
            p.wait(v, "yl0", p.tok("yl0"))

    scan_done = p.tok("yl0")

    for tc in range(ntc):
        for st2 in range(2):
            for dc in range(NCHUNK):
                b = dc % 2

                @p.on("sync")
                def _(sync, tc=tc, st2=st2, dc=dc, b=b):
                    cnt = (tc * 2 + st2) * NCHUNK + dc
                    if tc == 0 and st2 == 0 and dc == 0:
                        p.wait(sync, "yl0", p.tok("yl0"))
                    if cnt >= 2:
                        p.wait(sync, "smm", p.tok("smm"))
                        p.wait(sync, "gact", p.tok("gact"))
                    dma = sync.dma_start(out=yld[b][:],
                                         in_=ybh[dc * 128:(dc + 1) * 128, tc * 512:(tc + 1) * 512])
                    p.inc(dma, f"st{b}", 16)

                if st2 == 0:
                    @p.on("tensor")
                    def _(t, tc=tc, dc=dc, b=b):
                        p.wait(t, f"st{b}", p.tok(f"st{b}"))
                        if tc == 0 and dc == 0:
                            p.wait(t, "ev", p.tok("ev"))
                        ins = t.matmul(ps_s1[0:1, :], ones1[:], yld[b][:],
                                       start=(dc == 0), stop=(dc == NCHUNK - 1))
                        p.inc(ins, "smm")
                else:
                    @p.on("scalar")
                    def _(s, dc=dc, b=b):
                        p.wait(s, f"st{b}", p.tok(f"st{b}"))
                        p.wait(s, "smm", p.tok("smm"))
                        p.inc(s.activation(y2t[:], yld[b][:], AF.Square), "gact")

                    @p.on("tensor")
                    def _(t, dc=dc, b=b):
                        p.wait(t, "gact", p.tok("gact"))
                        ins = t.matmul(ps_s2[0:1, :], ones1[:], y2t[:],
                                       start=(dc == 0), stop=(dc == NCHUNK - 1))
                        p.inc(ins, "smm")

        @p.on("vector")
        def _(v, tc=tc):
            p.wait(v, "smm", p.tok("smm"))
            p.inc(v.tensor_copy(srow[0:1, tc * 512:(tc + 1) * 512], ps_s1[0:1, :]), "sev")
            p.inc(v.tensor_copy(sqrow[0:1, tc * 512:(tc + 1) * 512], ps_s2[0:1, :]), "sev")

    tmpr = p.sb("tmpr", 1, 512, F32)

    @p.on("vector")
    def _(v):
        p.wait(v, "sev", p.tok("sev"))
        p.inc(v.tensor_scalar(murow[:], srow[:], 1.0 / D, None, ALU.mult, ALU.bypass), "sev")
        p.wait(v, "sev")
        for qq in range(T // 512):
            sl = slice(qq * 512, (qq + 1) * 512)
            p.inc(v.tensor_mul(tmpr[:], murow[0:1, sl], murow[0:1, sl]), "sev")
            p.wait(v, "sev")
            p.inc(v.tensor_scalar(sqrow[0:1, sl], sqrow[0:1, sl], 1.0 / D, None,
                                  ALU.mult, ALU.bypass), "sev")
            p.wait(v, "sev")
            p.inc(v.tensor_sub(rsrow[0:1, sl], sqrow[0:1, sl], tmpr[:]), "sev")
            p.wait(v, "sev")
    var_tok = p.tok("sev")

    @p.on("scalar")
    def _(s):
        p.wait(s, "sev", var_tok)
        p.inc(s.activation(rsrow[:], rsrow[:], AF.Ln, bias=csb[0:1, 0:1]), "sc")
        p.wait(s, "sc")
        p.inc(s.activation(rsrow[:], rsrow[:], AF.Exp, scale=-0.5), "sc")
    rstd_tok = p.tok("sc")

    wg_tok = emit_weight_to(p, wG, D, D, WA, "Wg", sp, "cast2",
                            deps=(("mm", p.tok("mm")),))
    wo_tok = emit_weight_to(p, wO, D, D, WB, "Wo", sp, "cast2",
                            deps=(("mm", p.tok("mm")),))

    for tg in range(ntc):
        trange = slice(tg * 512, (tg + 1) * 512)
        @p.on("tensor")
        def _(t, tg=tg):
            p.wait(t, "sc", rstd_tok)
            if tg > 0:
                p.wait(t, "ev", p.tok("ev"))
            ins = t.matmul(ps_b1[:], onesw[:], murow[0:1, trange], start=True, stop=True)
            p.inc(ins, "gmm")
            ins = t.matmul(ps_b2[:], onesw[:], rsrow[0:1, trange], start=True, stop=True)
            p.inc(ins, "gmm")

        for dc in range(NCHUNK):
            @p.on("tensor")
            def _(t, tg=tg, dc=dc):
                if tg == 0 and dc == 0:
                    p.wait(t, wg_tok)
                p.wait(t, "gact", p.tok("gact"))
                for k in range(NCHUNK):
                    ins = t.matmul(ps_g[:], WA[:, k * D + dc * 128: k * D + dc * 128 + 128],
                                   xfm[k][:, trange], start=(k == 0), stop=(k == NCHUNK - 1))
                p.inc(ins, "gmm")

            @p.on("scalar")
            def _(s, tg=tg, dc=dc):
                p.wait(s, "gmm", p.tok("gmm"))
                p.wait(s, "ev", p.tok("ev"))
                p.inc(s.activation(gbf[:], ps_g[:], AF.Silu), "gact")

            b = dc % 2

            @p.on("sync")
            def _(sync, tg=tg, dc=dc, b=b):
                p.wait(sync, "ev", p.tok("ev"))
                dma = sync.dma_start(out=yld[b][:],
                                     in_=ybh[dc * 128:(dc + 1) * 128, trange])
                p.inc(dma, f"st{b}", 16)

            @p.on("vector")
            def _(v, tg=tg, dc=dc, b=b):
                p.wait(v, f"st{b}", p.tok(f"st{b}"))
                p.wait(v, "gmm", p.tok("gmm"))
                p.wait(v, "gact", p.tok("gact"))
                p.inc(v.tensor_sub(t1b[:], yld[b][:], ps_b1[:]), "ev")
                p.wait(v, "ev")
                p.inc(v.tensor_mul(t1b[:], t1b[:], ps_b2[:]), "ev")
                p.wait(v, "ev")
                p.inc(v.tensor_scalar(t1b[:], t1b[:], gams[:, dc:dc + 1], gbes[:, dc:dc + 1],
                                      ALU.mult, ALU.add), "ev")
                p.wait(v, "ev")
                p.inc(v.tensor_mul(ygt[dc], t1b[:], gbf[:]), "ev")

        yg_tok = p.tok("ev")
        for tsub in range(4):
            tt = tg * 4 + tsub
            ob = tsub % 2

            @p.on("sync")
            def _(sync, tt=tt, ob=ob):
                if tt == 0:
                    p.wait(sync, "ln_c0")
                    p.wait(sync, "ln_c1")
                if tt < 2:
                    p.wait(sync, "yl0", scan_done)
                p.wait(sync, f"x{ob}", p.tok(f"x{ob}"))
                dma = sync.dma_start(out=xrt[ob], in_=xcur[tt * 128:(tt + 1) * 128, :])
                p.inc(dma, f"r{ob}", 16)
            r_tok = p.tok(f"r{ob}")

            for dc2 in range(2):
                @p.on("tensor")
                def _(t, tg=tg, tsub=tsub, ob=ob, dc2=dc2, yg_tok=yg_tok):
                    if tg == 0 and tsub == 0 and dc2 == 0:
                        p.wait(t, wo_tok)
                    p.wait(t, "ev", yg_tok)
                    p.wait(t, "oev", p.tok("oev"))
                    for k in range(NCHUNK):
                        ins = t.matmul(ps_o[ob][:],
                                       ygt[k][:, tsub * 128:(tsub + 1) * 128],
                                       WB[:, k * D + dc2 * 512: k * D + dc2 * 512 + 512],
                                       start=(k == 0), stop=(k == NCHUNK - 1))
                    p.inc(ins, "omm")

                @p.on("vector")
                def _(v, ob=ob, dc2=dc2, r_tok=r_tok):
                    p.wait(v, "omm", p.tok("omm"))
                    p.wait(v, f"r{ob}", r_tok)
                    p.inc(v.tensor_add(xot[ob][:, dc2 * 512:(dc2 + 1) * 512],
                                       xrt[ob][:, dc2 * 512:(dc2 + 1) * 512],
                                       ps_o[ob][:]), "oev")

            @p.on("sync")
            def _(sync, tt=tt, ob=ob):
                p.wait(sync, "oev", p.tok("oev"))
                dma = sync.dma_start(out=xn[tt * 128:(tt + 1) * 128, :], in_=xot[ob])
                p.inc(dma, f"x{ob}", 16)

    @p.on("sync")
    def _(sync):
        p.wait(sync, "x0")
        p.wait(sync, "x1")

    p.finish()
    return p


def host_l0ab_weights(inp):
    heads = (np.arange(D) // HD)
    msel = (heads[None, :] == np.arange(H)[:, None]).astype(np.float32)
    return dict(
        WqT=np.ascontiguousarray(inp["L0_attn_b_Wq"].T),
        WvT=np.ascontiguousarray(inp["L0_attn_b_Wv"].T),
        WgT=np.ascontiguousarray(inp["L0_attn_b_Wg"].T),
        WoT=np.ascontiguousarray(inp["L0_attn_b_Wo"].T),
        beta=np.ascontiguousarray(inp["L0_attn_b_beta"].reshape(16, 1)),
        Msel=msel,
        gamc=np.ascontiguousarray(inp["L0_attn_b_ln_gamma"].reshape(NCHUNK, 128).T),
        gbec=np.ascontiguousarray(inp["L0_attn_b_ln_beta"].reshape(NCHUNK, 128).T),
        cscal=np.tile(np.array([[EPS, 1.0, -6.0, -4.0]], np.float32), (128, 1)),
    )

def build_l1ab(T, mode="add"):
    p = Prog("l1ab")
    xprev = p.inp("xprev", [T, D])
    part1 = p.inp("part1", [T, D])
    part2 = p.inp("part2", [T, D])
    wI = p.inp("Wi", [D, D])
    wF = p.inp("Wf", [D, D])
    wG = p.inp("Wg", [D, D])
    wO = p.inp("Wo", [D, D])
    cvi = p.inp("convi", [128, NCHUNK * 4])
    cvf = p.inp("convf", [128, NCHUNK * 4])
    gnw = p.inp("gnormc", [128, NCHUNK])
    cs = p.inp("cscal", [128, 4])
    xn = p.out("xn", [T, D])
    xcur = p.dram("xcur", [T, D], F32)
    xlnh = p.dram("xlnh", [T, D], BF16)
    ybh = p.dram("ybh", [D, T], BF16)

    ntc = T // 512

    xfm = [p.sb(f"xfm{c}", 128, T, BF16) for c in range(NCHUNK)]
    csb = p.sb("csb", 128, 4, F32)
    WA = p.sb("WA", 128, 8 * D, BF16)
    WB = p.sb("WB", 128, 8 * D, BF16)
    cvis = p.sb("cvis", 128, NCHUNK * 4, F32)
    cvfs = p.sb("cvfs", 128, NCHUNK * 4, F32)
    gnws = p.sb("gnws", 128, NCHUNK, F32)
    ich = p.sb("ich", 128, T, BF16)
    fch = p.sb("fch", 128, T, BF16)
    icv = p.sb("icv", 128, T, F32)
    fcv = icv
    fsg = p.sb("fsg", 128, T, BF16)
    isl = p.sb("isl", 128, T, BF16)
    iet = p.sb("iet", 128, T, BF16)
    ytl = p.sb("ytl", 128, T, BF16)
    yld = [p.sb(f"yld{i}", 128, 512, BF16) for i in range(2)]
    y2t = p.sb("y2t", 128, 512, BF16)
    sqrow = p.sb("sqrow", 1, T, F32)
    rsrow = sqrow
    ones1 = p.sb("ones1", 128, 1, BF16)
    onesw = p.sb("onesw", 1, 128, F32)
    if T >= 4096:
        ygt = [fsg[:, c * 512:(c + 1) * 512] for c in range(NCHUNK)]
    else:
        ygt = [p.sb(f"ygtx{c}", 128, 512, BF16)[:] for c in range(NCHUNK)]
    gbf = p.sb("gbf", 128, 512, BF16)
    t1b = p.sb("t1b", 128, 512, F32)
    if T >= 4 * D:
        xrt = [icv[:, i * D:(i + 1) * D] for i in range(2)]
        xot = [icv[:, (2 + i) * D:(3 + i) * D] for i in range(2)]
    else:
        xrt = [p.sb(f"xrt{i}", 128, D, F32)[:] for i in range(2)]
        xot = [p.sb(f"xot{i}", 128, D, F32)[:] for i in range(2)]

    pool = [p.ps(f"pp{i}", 128, 512) for i in range(5)]
    ps_v = pool[0]
    ps_s2 = pool[1]
    ps_b2 = pool[1]
    ps_g = pool[2]
    ps_o = [pool[3], pool[4]]

    for sname in ("ld", "cast", "cast2", "mm", "ev", "sc", "scan",
                  "st0", "st1", "yl0", "smm", "sev", "gmm", "gact",
                  "omm", "oev", "r0", "r1", "x0", "x1"):
        p.sem(sname)
    p.begin()

    @p.on("sync")
    def _(sync):
        p.inc(sync.dma_start(out=csb[:], in_=cs), "ld", 16)
        p.inc(sync.dma_start(out=cvis[:], in_=cvi), "ld", 16)
        p.inc(sync.dma_start(out=cvfs[:], in_=cvf), "ld", 16)
        p.inc(sync.dma_start(out=gnws[:], in_=gnw), "ld", 16)
    ld0 = p.tok("ld")

    @p.on("vector")
    def _(v):
        p.wait(v, "ld", ld0)
        p.inc(v.memset(ones1[:], 1.0), "cast")
        p.inc(v.memset(onesw[:], 1.0), "cast")

    sp = StagePair(p, "stg", D)
    wi_tok = emit_weight_to(p, wI, D, D, WA, "Wi", sp, "cast")
    wf_tok = emit_weight_to(p, wF, D, D, WB, "Wf", sp, "cast")

    @p.on("scalar")
    def _(s):
        p.wait(s, "ld", ld0)
    emit_prologue(p, xprev, part1, part2, mode, T, xcur, xlnh, xfm, csb[:, 0:1], "ln", nbuf=1)
    fm_tok = p.tok("ln_f")

    for dc in range(NCHUNK):
        for which, Wt, dst, wtok in (("i", WA, ich, wi_tok), ("f", WB, fch, wf_tok)):
            for tc in range(ntc):
                @p.on("tensor")
                def _(t, dc=dc, tc=tc, Wt=Wt, wtok=wtok):
                    if dc == 0 and tc == 0:
                        p.wait(t, "ln_f", fm_tok)
                        p.wait(t, wtok)
                    p.wait(t, "ev", p.tok("ev"))
                    for k in range(NCHUNK):
                        ins = t.matmul(ps_v[:], Wt[:, k * D + dc * 128: k * D + dc * 128 + 128],
                                       xfm[k][:, tc * 512:(tc + 1) * 512],
                                       start=(k == 0), stop=(k == NCHUNK - 1))
                    p.inc(ins, "mm")

                @p.on("vector")
                def _(v, dc=dc, tc=tc, dst=dst):
                    p.wait(v, "mm", p.tok("mm"))
                    if dc > 0 and tc == 0:
                        p.wait(v, "scan", p.tok("scan"))
                    p.inc(v.tensor_copy(dst[:, tc * 512:(tc + 1) * 512], ps_v[:]), "ev")

        def emit_conv(v, raw, wc, dc):
            p.inc(v.tensor_scalar(icv[:], raw[:], wc[:, dc * 4 + 3: dc * 4 + 4], None,
                                  ALU.mult, ALU.bypass), "ev")
            for k in (2, 1, 0):
                sh = 3 - k
                p.wait(v, "ev")
                p.inc(v.scalar_tensor_tensor(icv[:, sh:T], raw[:, 0:T - sh],
                                             wc[:, dc * 4 + k: dc * 4 + k + 1],
                                             icv[:, sh:T], ALU.mult, ALU.add), "ev")

        @p.on("vector")
        def _(v, dc=dc):
            p.wait(v, "ev", p.tok("ev"))
            emit_conv(v, ich, cvis, dc)

        @p.on("scalar")
        def _(s, dc=dc):
            p.wait(s, "ev", p.tok("ev"))
            p.inc(s.activation(isl[:], icv[:], AF.Silu), "sc")

        @p.on("vector")
        def _(v, dc=dc):
            p.wait(v, "sc", p.tok("sc"))
            emit_conv(v, fch, cvfs, dc)

        @p.on("scalar")
        def _(s, dc=dc):
            p.wait(s, "ev", p.tok("ev"))
            p.inc(s.activation(fsg[:], icv[:], AF.Sigmoid), "sc")

        @p.on("vector")
        def _(v, dc=dc):
            p.wait(v, "sc", p.tok("sc"))
            p.inc(v.tensor_mul(iet[:], isl[:], fsg[:]), "ev")
            p.wait(v, "ev")
            p.inc(v.tensor_sub(iet[:], isl[:], iet[:]), "ev")
            p.wait(v, "ev")
            p.inc(v.tensor_tensor_scan(ytl[:], fsg[:], iet[:], 0.0,
                                       ALU.mult, ALU.add), "scan")

        @p.on("sync")
        def _(sync, dc=dc):
            p.wait(sync, "scan", p.tok("scan"))
            dma = sync.dma_start(out=ybh[dc * 128:(dc + 1) * 128, :], in_=ytl[:])
            p.inc(dma, "yl0", 16)

        @p.on("vector")
        def _(v, dc=dc):
            p.wait(v, "yl0", p.tok("yl0"))

    scan_done = p.tok("yl0")

    for tc in range(ntc):
        for dc in range(NCHUNK):
            b = dc % 2

            @p.on("sync")
            def _(sync, tc=tc, dc=dc, b=b):
                if tc == 0 and dc == 0:
                    p.wait(sync, "yl0", p.tok("yl0"))
                if tc * NCHUNK + dc >= 2:
                    p.wait(sync, "gact", p.tok("gact"))
                dma = sync.dma_start(out=yld[b][:],
                                     in_=ybh[dc * 128:(dc + 1) * 128, tc * 512:(tc + 1) * 512])
                p.inc(dma, f"st{b}", 16)

            @p.on("scalar")
            def _(s, dc=dc, b=b):
                p.wait(s, f"st{b}", p.tok(f"st{b}"))
                p.wait(s, "smm", p.tok("smm"))
                p.inc(s.activation(y2t[:], yld[b][:], AF.Square), "gact")

            @p.on("tensor")
            def _(t, tc=tc, dc=dc):
                p.wait(t, "gact", p.tok("gact"))
                if tc == 0 and dc == 0:
                    p.wait(t, "ev", p.tok("ev"))
                ins = t.matmul(ps_s2[0:1, :], ones1[:], y2t[:],
                               start=(dc == 0), stop=(dc == NCHUNK - 1))
                p.inc(ins, "smm")

        @p.on("vector")
        def _(v, tc=tc):
            p.wait(v, "smm", p.tok("smm"))
            p.inc(v.tensor_copy(sqrow[0:1, tc * 512:(tc + 1) * 512], ps_s2[0:1, :]), "sev")

    @p.on("vector")
    def _(v):
        p.wait(v, "sev", p.tok("sev"))
        p.inc(v.tensor_scalar(sqrow[:], sqrow[:], 1.0 / D, None, ALU.mult, ALU.bypass), "sev")
    var_tok = p.tok("sev")

    @p.on("scalar")
    def _(s):
        p.wait(s, "sev", var_tok)
        p.inc(s.activation(rsrow[:], sqrow[:], AF.Ln, bias=csb[0:1, 0:1]), "sc")
        p.wait(s, "sc")
        p.inc(s.activation(rsrow[:], rsrow[:], AF.Exp, scale=-0.5), "sc")
    rstd_tok = p.tok("sc")

    wg_tok = emit_weight_to(p, wG, D, D, WA, "Wg2", sp, "cast2",
                            deps=(("mm", p.tok("mm")),))
    wo_tok = emit_weight_to(p, wO, D, D, WB, "Wo2", sp, "cast2",
                            deps=(("mm", p.tok("mm")),))

    for tg in range(ntc):
        trange = slice(tg * 512, (tg + 1) * 512)

        @p.on("tensor")
        def _(t, tg=tg):
            p.wait(t, "sc", rstd_tok)
            if tg > 0:
                p.wait(t, "ev", p.tok("ev"))
            ins = t.matmul(ps_b2[:], onesw[:], rsrow[0:1, trange], start=True, stop=True)
            p.inc(ins, "gmm")

        for dc in range(NCHUNK):
            @p.on("tensor")
            def _(t, tg=tg, dc=dc):
                if tg == 0 and dc == 0:
                    p.wait(t, wg_tok)
                p.wait(t, "gact", p.tok("gact"))
                for k in range(NCHUNK):
                    ins = t.matmul(ps_g[:], WA[:, k * D + dc * 128: k * D + dc * 128 + 128],
                                   xfm[k][:, trange], start=(k == 0), stop=(k == NCHUNK - 1))
                p.inc(ins, "gmm")

            @p.on("scalar")
            def _(s, tg=tg, dc=dc):
                p.wait(s, "gmm", p.tok("gmm"))
                p.wait(s, "ev", p.tok("ev"))
                p.inc(s.activation(gbf[:], ps_g[:], AF.Silu), "gact")

            b = dc % 2

            @p.on("sync")
            def _(sync, tg=tg, dc=dc, b=b):
                p.wait(sync, "ev", p.tok("ev"))
                dma = sync.dma_start(out=yld[b][:],
                                     in_=ybh[dc * 128:(dc + 1) * 128, trange])
                p.inc(dma, f"st{b}", 16)

            @p.on("vector")
            def _(v, tg=tg, dc=dc, b=b):
                p.wait(v, f"st{b}", p.tok(f"st{b}"))
                p.wait(v, "gmm", p.tok("gmm"))
                p.wait(v, "gact", p.tok("gact"))
                p.inc(v.tensor_mul(t1b[:], yld[b][:], ps_b2[:]), "ev")
                p.wait(v, "ev")
                p.inc(v.tensor_scalar(t1b[:], t1b[:], gnws[:, dc:dc + 1], None,
                                      ALU.mult, ALU.bypass), "ev")
                p.wait(v, "ev")
                p.inc(v.tensor_mul(ygt[dc], t1b[:], gbf[:]), "ev")

        yg_tok = p.tok("ev")
        for tsub in range(4):
            tt = tg * 4 + tsub
            ob = tsub % 2

            @p.on("sync")
            def _(sync, tt=tt, ob=ob):
                if tt == 0:
                    p.wait(sync, "ln_c0")
                    p.wait(sync, "ln_c1")
                if tt < 2:
                    p.wait(sync, "yl0", scan_done)
                p.wait(sync, f"x{ob}", p.tok(f"x{ob}"))
                dma = sync.dma_start(out=xrt[ob], in_=xcur[tt * 128:(tt + 1) * 128, :])
                p.inc(dma, f"r{ob}", 16)
            r_tok = p.tok(f"r{ob}")

            for dc2 in range(2):
                @p.on("tensor")
                def _(t, tg=tg, tsub=tsub, ob=ob, dc2=dc2, yg_tok=yg_tok):
                    if tg == 0 and tsub == 0 and dc2 == 0:
                        p.wait(t, wo_tok)
                    p.wait(t, "ev", yg_tok)
                    p.wait(t, "oev", p.tok("oev"))
                    for k in range(NCHUNK):
                        ins = t.matmul(ps_o[ob][:],
                                       ygt[k][:, tsub * 128:(tsub + 1) * 128],
                                       WB[:, k * D + dc2 * 512: k * D + dc2 * 512 + 512],
                                       start=(k == 0), stop=(k == NCHUNK - 1))
                    p.inc(ins, "omm")

                @p.on("vector")
                def _(v, ob=ob, dc2=dc2, r_tok=r_tok):
                    p.wait(v, "omm", p.tok("omm"))
                    p.wait(v, f"r{ob}", r_tok)
                    p.inc(v.tensor_add(xot[ob][:, dc2 * 512:(dc2 + 1) * 512],
                                       xrt[ob][:, dc2 * 512:(dc2 + 1) * 512],
                                       ps_o[ob][:]), "oev")

            @p.on("sync")
            def _(sync, tt=tt, ob=ob):
                p.wait(sync, "oev", p.tok("oev"))
                dma = sync.dma_start(out=xn[tt * 128:(tt + 1) * 128, :], in_=xot[ob])
                p.inc(dma, f"x{ob}", 16)

    @p.on("sync")
    def _(sync):
        p.wait(sync, "x0")
        p.wait(sync, "x1")

    p.finish()
    return p


def host_l1ab_weights(inp):
    return dict(
        Wi=np.ascontiguousarray(inp["L1_attn_b_Wi"]),
        Wf=np.ascontiguousarray(inp["L1_attn_b_Wf"]),
        Wg=np.ascontiguousarray(inp["L1_attn_b_Wg"]),
        Wo=np.ascontiguousarray(inp["L1_attn_b_Wo"]),
        convi=np.ascontiguousarray(inp["L1_attn_b_conv_i"].reshape(NCHUNK, 128, 4).transpose(1, 0, 2).reshape(128, NCHUNK * 4)),
        convf=np.ascontiguousarray(inp["L1_attn_b_conv_f"].reshape(NCHUNK, 128, 4).transpose(1, 0, 2).reshape(128, NCHUNK * 4)),
        gnormc=np.ascontiguousarray(inp["L1_attn_b_g_norm_weight"].reshape(NCHUNK, 128).T),
        cscal=np.tile(np.array([[EPS, 1.0, -6.0, -4.0]], np.float32), (128, 1)),
    )

def build_l1aa(T, mode="round", stage=99, nqt=None, lite=0):
    p = Prog("l1aa")
    xprev = p.inp("xprev", [T, D])
    part1 = p.inp("part1", [T, D])
    part2 = p.inp("part2", [T, D])
    wQKV = p.inp("qkvT", [D, 3 * 512])
    wOUT = p.inp("outT", [512, D])
    ctm = p.inp("Ctm", [T, 512], BF16)
    stm = p.inp("Stm", [T, 512], BF16)
    tmk = p.inp("Tmskb", [128, 384], BF16)
    cs = p.inp("cscal", [128, 4])
    xc_out = p.out("xc", [T, D])
    pout = p.out("pout", [T, D])
    xlnh = p.dram("xlnh", [T, D], BF16)
    qkh = p.dram("qkh", [T, D], BF16)

    ntt = T // 128

    xfm = [p.sb(f"xfm{c}", 128, T, BF16) for c in range(NCHUNK)]
    qfm = xfm[0:4]
    kfm = xfm[4:8]
    csb = p.sb("csb", 128, 4, F32)
    WQKV = p.sb("WQKV", 128, NCHUNK * 1536, BF16)
    WO = p.sb("WO", 128, 4 * D, BF16)
    cst = [p.sb(f"cst{i}", 128, 1024, BF16) for i in range(2)]
    qrt = p.sb("qrt", 128, 512, BF16)
    krt = p.sb("krt", 128, 512, BF16)
    rtmp = p.sb("rtmp", 128, 256, F32)
    vaug = p.sb("vaug", 128, ntt * 520, BF16)
    Ofm = [p.sb(f"Ofm{c}", 128, T, BF16) for c in range(4)]
    expt = [p.sb(f"expt{i}", 128, 128, BF16) for i in range(2)]
    dnm = p.sb("dnm", 1, 128, F32)
    bcs = p.sb("bcs", 64, 128, F32)
    onesw = p.sb("onesw", 1, 64, F32)
    pot = [p.sb("potx", 128, D, F32)] * 2

    ps_q = p.ps("ps_q", 128, 512)
    ps_k = p.ps("ps_k", 128, 512)
    ps_v = p.ps("ps_v", 128, 512)
    ps_sml = p.ps("ps_sml", 128, 512)
    ps_pvh = p.ps("ps_pvh", 128, 512)
    ps_po = [p.ps(f"ps_po{i}", 128, 512) for i in range(2)]
    ps_se = [ps_q, ps_k]

    for sname in ("ld", "cast", "mm", "ro", "qs0", "qs1", "vev",
                  "cs0", "cs1", "tr", "sev", "aff", "pmm", "pev",
                  "o0", "o1", "r0", "r1"):
        p.sem(sname)
    p.begin()

    tms = p.sb("tms", 128, 384, BF16)

    @p.on("sync")
    def _(sync):
        p.inc(sync.dma_start(out=csb[:], in_=cs), "ld", 16)
        p.inc(sync.dma_start(out=tms[:], in_=tmk), "ld", 16)
    sp = StagePair(p, "stg", 1536)
    qkv_tok = emit_weight_to(p, wQKV, D, 1536, WQKV, "Wqkv", sp, "cast")
    spo = StagePair(p, "sto", D)
    wo_tok = emit_weight_to(p, wOUT, 512, D, WO, "Wo", spo, "cast")
    ld0 = p.tok("ld")

    @p.on("vector")
    def _(v):
        p.wait(v, "ld", ld0)
        p.inc(v.memset(onesw[:], 1.0), "cast")
        p.inc(v.memset(vaug[:], 1.0), "cast")
    vini_tok = p.tok("cast")

    @p.on("scalar")
    def _(s):
        p.wait(s, "ld", ld0)
    emit_prologue(p, xprev, part1, part2, mode, T, xc_out, xlnh, xfm, csb[:, 0:1], "ln")
    fm_tok = p.tok("ln_f")

    if stage < 1:
        p.finish()
        return p
    def rope(v, ps, dst, cs_b):
        C = cst[cs_b][:, 0:512]
        S = cst[cs_b][:, 512:1024]
        p.wait(v, "ro")
        p.inc(v.tensor_mul(dst[:, 0:512:2], ps[:, 0:512:2], C[:, 0:512:2]), "ro")
        p.inc(v.tensor_mul(rtmp[:], ps[:, 1:512:2], S[:, 0:512:2]), "ro")
        p.wait(v, "ro")
        p.inc(v.tensor_sub(dst[:, 0:512:2], dst[:, 0:512:2], rtmp[:]), "ro")
        p.inc(v.tensor_mul(dst[:, 1:512:2], ps[:, 1:512:2], C[:, 1:512:2]), "ro")
        p.wait(v, "ro")
        p.inc(v.tensor_mul(rtmp[:], ps[:, 0:512:2], S[:, 1:512:2]), "ro")
        p.wait(v, "ro")
        p.inc(v.tensor_add(dst[:, 1:512:2], dst[:, 1:512:2], rtmp[:]), "ro")

    for tt in range(ntt):
        b = tt % 2
        r = slice(tt * 128, (tt + 1) * 128)

        @p.on("scalar")
        def _(s, tt=tt, b=b, r=r):
            if tt >= 2:
                p.wait(s, "ro", p.tok("ro"))
            dma = s.dma_start(out=cst[b][:, 0:512], in_=ctm[r, :])
            p.inc(dma, f"cs{b}", 16)
            dma = s.dma_start(out=cst[b][:, 512:1024], in_=stm[r, :])
            p.inc(dma, f"cs{b}", 16)

        @p.on("tensor")
        def _(t, tt=tt, b=b):
            if tt == 0:
                p.wait(t, "ln_f", fm_tok)
                p.wait(t, qkv_tok)
                p.wait(t, "cast", vini_tok)
            p.wait(t, "ro", p.tok("ro"))
            p.wait(t, "vev", p.tok("vev"))
            for part, ps in ((0, ps_q), (1, ps_k), (2, ps_v)):
                for k in range(NCHUNK):
                    ins = t.matmul(ps[:], xfm[k][:, tt * 128:(tt + 1) * 128],
                                   WQKV[:, k * 1536 + part * 512: k * 1536 + part * 512 + 512],
                                   start=(k == 0), stop=(k == NCHUNK - 1))
            p.inc(ins, "mm")

        @p.on("vector")
        def _(v, tt=tt, b=b):
            p.wait(v, "mm", p.tok("mm"))
            p.wait(v, f"cs{b}", p.tok(f"cs{b}"))
            if tt >= 1:
                p.wait(v, "qs0", p.tok("qs0"))
                p.wait(v, "qs1", p.tok("qs1"))
            rope(v, ps_q, qrt, b)
            rope(v, ps_k, krt, b)
            for h in range(8):
                p.inc(v.tensor_copy(vaug[:, tt * 520 + 65 * h: tt * 520 + 65 * h + 64],
                                    ps_v[:, 64 * h: 64 * h + 64]), "vev")

        @p.on("sync")
        def _(sync, tt=tt, b=b, r=r):
            p.wait(sync, "ro", p.tok("ro"))
            dma = sync.dma_start(out=qkh[r, 0:512], in_=qrt[:])
            p.inc(dma, f"qs{b}", 16)
            dma = sync.dma_start(out=qkh[r, 512:1024], in_=krt[:])
            p.inc(dma, f"qs{b}", 16)

    if stage < 2:
        p.finish()
        return p
    @p.on("scalar")
    def _(s):
        p.wait(s, "mm", p.tok("mm"))
        p.wait(s, "qs0")
        p.wait(s, "qs1")
        for c in range(NCHUNK):
            dma = s.dma_start_transpose(out=xfm[c][:], in_=qkh[:, c * 128:(c + 1) * 128])
            p.inc(dma, "tr", 16)
    tr_tok = p.tok("tr")

    if stage < 3:
        p.finish()
        return p
    ntt_a = ntt if nqt is None else nqt
    def ktiles_of(bi):
        if bi == 0:
            return [(0, "causal")]
        if bi == 1:
            return [(0, "full"), (1, "causal")]
        kts = [(0, "glob")]
        kts.append((bi - 1, "full" if bi % 2 == 1 else "window"))
        kts.append((bi, "causal"))
        return kts

    for qt in range(ntt_a):
        for h in range(8):
            hc = h // 2
            hb = (h % 2) * 64
            kl = ktiles_of(qt)
            nk = len(kl)
            for ki, (kt, mask) in enumerate(kl):
                eb = ki % 2

                @p.on("tensor")
                def _(t, qt=qt, h=h, hc=hc, hb=hb, kt=kt, eb=eb):
                    if qt == 0 and h == 0:
                        p.wait(t, "tr", tr_tok)
                        p.wait(t, "vev", p.tok("vev"))
                    p.wait(t, "aff", p.tok("aff"))
                    ins = t.matmul(ps_se[eb][:, 0:128],
                                   kfm[hc][hb:hb + 64, kt * 128:(kt + 1) * 128],
                                   qfm[hc][hb:hb + 64, qt * 128:(qt + 1) * 128],
                                   start=True, stop=True)
                    p.inc(ins, "mm")

                if lite >= 3:
                    continue
                @p.on("scalar")
                def _(s, eb=eb):
                    p.wait(s, "mm", p.tok("mm"))
                    p.wait(s, "aff", p.tok("aff"))
                    p.inc(s.activation(expt[eb][:], ps_se[eb][:, 0:128], AF.Exp,
                                       scale=0.125, bias=csb[:, 2:3]), "sev")

                if mask != "full":
                    @p.on("vector")
                    def _(v, eb=eb, mask=mask):
                        p.wait(v, "sev", p.tok("sev"))
                        sl = {"causal": 0, "window": 1, "glob": 2}[mask]
                        p.inc(v.tensor_mul(expt[eb][:], expt[eb][:],
                                           tms[:, sl * 128:(sl + 1) * 128]), "aff")

                if lite >= 2:
                    continue
                @p.on("tensor")
                def _(t, qt=qt, h=h, kt=kt, eb=eb, ki=ki, nk=nk):
                    p.wait(t, "aff", p.tok("aff"))
                    p.wait(t, "sev", p.tok("sev"))
                    if ki == 0:
                        p.wait(t, "pev", p.tok("pev"))
                    ins = t.matmul(ps_pvh[0:65, 0:128],
                                   vaug[:, kt * 520 + 65 * h: kt * 520 + 65 * h + 65],
                                   expt[eb][:],
                                   start=(ki == 0), stop=(ki == nk - 1))
                    p.inc(ins, "pmm")

            if lite >= 1:
                continue
            @p.on("vector")
            def _(v, qt=qt, h=h, hc=hc, hb=hb):
                p.wait(v, "pmm", p.tok("pmm"))
                p.inc(v.tensor_copy(dnm[:], ps_pvh[64:65, 0:128]), "pev")
                p.wait(v, "pev")
                p.inc(v.reciprocal(dnm[:], dnm[:]), "pev")

            @p.on("tensor")
            def _(t, qt=qt, h=h):
                p.wait(t, "pev", p.tok("pev"))
                ins = t.matmul(ps_v[0:64, 0:128], onesw[:], dnm[:], start=True, stop=True)
                p.inc(ins, "pmm")

            @p.on("vector")
            def _(v, qt=qt, h=h, hc=hc, hb=hb):
                p.wait(v, "pmm", p.tok("pmm"))
                p.inc(v.tensor_copy(bcs[:], ps_v[0:64, 0:128]), "pev")
                p.wait(v, "pev")
                p.inc(v.tensor_mul(Ofm[hc][hb:hb + 64, qt * 128:(qt + 1) * 128],
                                   ps_pvh[0:64, 0:128], bcs[:]), "pev")

    if stage < 4:
        p.finish()
        return p
    of_tok = p.tok("pev")
    for tt in range(ntt):
        ob = tt % 2
        r = slice(tt * 128, (tt + 1) * 128)
        for dc2 in range(2):
            @p.on("tensor")
            def _(t, tt=tt, ob=ob, dc2=dc2, of_tok=of_tok):
                if tt == 0 and dc2 == 0:
                    p.wait(t, wo_tok)
                    p.wait(t, "pev", of_tok)
                p.wait(t, "o" + str(ob), p.tok("o" + str(ob)))
                p.wait(t, "pev", p.tok("pev"))
                for k in range(4):
                    ins = t.matmul(ps_po[ob][:],
                                   Ofm[k][:, tt * 128:(tt + 1) * 128],
                                   WO[:, k * D + dc2 * 512: k * D + dc2 * 512 + 512],
                                   start=(k == 0), stop=(k == 3))
                p.inc(ins, "pmm")

            @p.on("vector")
            def _(v, ob=ob, dc2=dc2):
                p.wait(v, "pmm", p.tok("pmm"))
                p.wait(v, "o0", p.tok("o0"))
                p.wait(v, "o1", p.tok("o1"))
                p.inc(v.tensor_copy(pot[ob][:, dc2 * 512:(dc2 + 1) * 512], ps_po[ob][:]), "pev")

        @p.on("sync")
        def _(sync, tt=tt, ob=ob, r=r):
            p.wait(sync, "pev", p.tok("pev"))
            dma = sync.dma_start(out=pout[r, :], in_=pot[ob][:])
            p.inc(dma, f"o{ob}", 16)

    @p.on("sync")
    def _(sync):
        p.wait(sync, "o0")
        p.wait(sync, "o1")
        p.wait(sync, "ln_c0")
        p.wait(sync, "ln_c1")

    p.finish()
    return p


def host_l1aa_weights(inp, j):
    qkv_w = inp["L1_attn_a_qkv_w"]
    rows = []
    for part in range(3):
        rows.append(qkv_w[part * D + 512 * j: part * D + 512 * j + 512, :])
    qkvT = np.ascontiguousarray(np.concatenate(rows, 0).T)
    outT = np.ascontiguousarray(inp["L1_attn_a_out_w"].T[512 * j: 512 * j + 512, :])
    return dict(qkvT=qkvT, outT=outT)


def host_l1aa_consts(T):
    hd = HD
    inv = 1.0 / (10000.0 ** (np.arange(0, hd, 2, dtype=np.float64) / hd))
    pos = np.arange(T, dtype=np.float64)
    freqs = np.outer(pos, inv)
    emb = np.concatenate([freqs, freqs], -1)
    c = np.cos(emb).astype(BF)
    s = np.sin(emb).astype(BF)
    pidx = np.arange(128)[:, None]
    fidx = np.arange(128)[None, :]
    mca = (pidx <= fidx)
    mwi = (pidx >= fidx)
    mgl = (pidx < 4) * np.ones((1, 128), bool)
    tmb = np.concatenate([mca, mwi, mgl], 1).astype(BF)
    return dict(Ctm=np.ascontiguousarray(np.tile(c, (1, 8))),
                Stm=np.ascontiguousarray(np.tile(s, (1, 8))),
                Tmskb=np.ascontiguousarray(tmb))

def build_moe(T, mode="add", stage=99):
    p = Prog("moe", scratch=49152)
    CAP = T // 8
    KSEL = T // 4
    NST = CAP // 128
    BIG = 1.0e5
    xprev = p.inp("xprev", [T, D])
    part1 = p.inp("part1", [T, D])
    part2 = p.inp("part2", [T, D])
    gw = p.inp("gatew", [D, NEXP])
    w1 = p.inp("w1", [NEXP, D, DFF])
    w2 = p.inp("w2", [NEXP, DFF, D])
    slotb = p.inp("slotb", [8, 1])
    ecap1 = p.inp("ecap1", [8, 1])
    idf = p.inp("idf32", [128, 128])
    cs = p.inp("cscal", [128, 4])
    tid = p.inp("tokid", [T, 2], BF16)
    xc_out = p.out("xc", [T, D])
    pm = p.out("pm", [T, D])
    xlnh2 = p.dram("xlnh2", [T, 1034], BF16)
    xse = [p.dram(f"xse{e}", [CAP, 1034], BF16) for e in range(NEXP)]

    ntt = T // 128

    xfm = [p.sb(f"xfm{c}", 128, T, BF16) for c in range(NCHUNK)]
    csb = p.sb("csb", 128, 4, F32)
    gws = p.sb("gws", 128, NCHUNK * NEXP, F32)
    gwb = p.sb("gwb", 128, NCHUNK * NEXP, BF16)
    idfs = p.sb("idfs", 128, 128, F32)
    slb = p.sb("slb", 8, 1, F32)
    ecp = p.sb("ecp", 8, 1, F32)
    prtile = [p.sb(f"prt{i}", 128, 8, F32) for i in range(2)]
    prbf = [p.sb(f"prb{i}", 128, 8, BF16) for i in range(2)]
    nmx = [p.sb(f"nmx{i}", 128, 1, F32) for i in range(2)]
    ssum = [p.sb(f"ssum{i}", 128, 1, F32) for i in range(2)]
    prbt = p.sb("prbt", 8, T, F32)
    junk8 = p.sb("junk8", 8, T, F32)
    zrow8 = junk8
    msk8 = p.sb("msk8", 8, T, F32)
    dst8 = p.sb("dst8", 8, T, F32)
    tmp8 = junk8
    lo8 = p.sb("lo8", 8, 1, F32)
    hi8 = p.sb("hi8", 8, 1, F32)
    th8 = p.sb("th8", 8, 1, F32)
    cn8 = p.sb("cn8", 8, 1, F32)
    mm8 = p.sb("mm8", 8, 1, F32)
    dd8 = p.sb("dd8", 8, 1, F32)
    bia8 = p.sb("bia8", 8, 1, F32)
    idxtm = p.sb("idxtm", 128, ntt * 8, I32)
    idxf = p.sb("idxf", 128, ntt * 8, F32)
    xsfm = [p.sb(f"xsfm{c}", 128, CAP, BF16) for c in range(NCHUNK)]
    gcol = p.sb("gcol", 128, NST, F32)
    if T >= 4096:
        W1g = xfm[1]
        W2g = xfm[2]
        _w1f = xfm[5][:].bitcast(F32)
        _w2f = xfm[6][:].bitcast(F32)
        _w3f = xfm[7][:].bitcast(F32)
    else:
        W1g = p.sb("W1g", 128, 4096, BF16)
        W2g = p.sb("W2g", 128, 4096, BF16)
        _w1f = p.sb("wbA", 128, 2048, F32)[:]
        _w2f = p.sb("wbB", 128, 2048, F32)[:]
        _w3f = p.sb("wbC", 128, 2048, F32)[:]
    acc = [[(_w1f if i == 0 else _w2f)[:, jj * D:(jj + 1) * D] for jj in range(2)] for i in range(2)]
    pmt = [_w3f[:, i * D:(i + 1) * D] for i in range(2)]
    hbuf = xfm[0][:, 0:4 * CAP]
    h2a = [xfm[3][:].bitcast(F32), xfm[4][:].bitcast(F32)]
    xgld = [p.sb(f"xgld{i}", 128, 1034, BF16) for i in range(2)]
    tid2 = p.sb("tid2", 128, 2, F32)
    tidi = p.sb("tidi", 128, 1, I32)
    zot = p.sb("zot", 128, D, F32)
    h2wt2 = [p.sb(f"h2wtt{i}", 128, D, F32) for i in range(2)]

    ps_g = p.ps("ps_g", 128, 512)
    ps_h = p.ps("ps_h", 128, 512)
    ps_h2 = p.ps("ps_h2", 128, 512)

    for sname in (["ld", "cast", "gmm", "gev", "sax", "tev", "bis",
                  "cmp", "xg0", "xg1", "trx", "gld", "wld", "mm", "hact",
                  "aev", "wb0", "wb1", "h2s", "pm0", "pm1", "gs0", "gs1",
                  "tidc", "tidw", "zf", "zfv"]
                  + [f"se{e}" for e in range(NEXP)]):
        p.sem(sname)
    p.begin()

    @p.on("sync")
    def _(sync):
        p.inc(sync.dma_start(out=csb[:], in_=cs), "ld", 16)
        p.inc(sync.dma_start(out=gws[:], in_=gw.rearrange("(c p) n -> p c n", p=128)), "ld", 16)
        p.inc(sync.dma_start(out=idfs[:], in_=idf), "ld", 16)
        p.inc(sync.dma_start(out=slb[:], in_=slotb), "ld", 16)
        p.inc(sync.dma_start(out=ecp[:], in_=ecap1), "ld", 16)
    ld0 = p.tok("ld")

    breg = p.nc.gpsimd.alloc_register("bndreg")
    tbreg = p.nc.gpsimd.alloc_register("tbndreg")

    @p.on("gpsimd")
    def _(g):
        g.reg_mov(breg, CAP - 1)
        g.reg_mov(tbreg, T - 1)

    @p.on("vector")
    def _(v):
        p.inc(v.memset(zot[:], 0.0), "zfv")

    @p.on("sync")
    def _(sync):
        p.wait(sync, "zfv", 1)
        for tt in range(T // 128):
            dma = sync.dma_start(out=pm[tt * 128:(tt + 1) * 128, :], in_=zot[:])
            p.inc(dma, "zf", 16)

    @p.on("vector")
    def _(v):
        p.wait(v, "ld", ld0)
        p.inc(v.tensor_copy(gwb[:], gws[:]), "cast")
        p.inc(v.memset(lo8[:], 0.0), "cast")
        p.inc(v.memset(hi8[:], 1.0), "cast")
        p.inc(v.memset(th8[:], 0.5), "cast")
        p.inc(v.tensor_sub(bia8[:], ecp[:], slb[:]), "cast")
    ini_tok = p.tok("cast")

    @p.on("scalar")
    def _(s):
        p.wait(s, "ld", ld0)
    emit_prologue(p, xprev, part1, part2, mode, T, xc_out, xlnh2[:, 10:1034], xfm,
                  csb[:, 0:1], "ln", nbuf=1)

    @p.on("sync")
    def _(sync):
        dma = sync.dma_start(out=xlnh2[:, 8:10], in_=tid)
        p.inc(dma, "gs0", 16)
    fm_tok = p.tok("ln_f")

    if stage < 1:
        p.finish()
        return p
    for tt in range(ntt):
        b = tt % 2
        r = slice(tt * 128, (tt + 1) * 128)

        @p.on("tensor")
        def _(t, tt=tt):
            if tt == 0:
                p.wait(t, "ln_f", fm_tok)
                p.wait(t, "cast", ini_tok)
            p.wait(t, "gev", p.tok("gev"))
            for k in range(NCHUNK):
                ins = t.matmul(ps_g[:, 0:8], xfm[k][:, tt * 128:(tt + 1) * 128],
                               gwb[:, NEXP * k: NEXP * k + 8],
                               start=(k == 0), stop=(k == NCHUNK - 1))
            p.inc(ins, "gmm")

        @p.on("vector")
        def _(v, tt=tt, b=b):
            p.wait(v, "gmm", p.tok("gmm"))
            if tt >= 2:
                p.wait(v, "sax", p.tok("sax"))
                p.wait(v, f"gs{b}", p.tok(f"gs{b}"))
            p.inc(v.reduce_max(nmx[b][:], ps_g[:, 0:8], axis=AX.X, negate=True), "gev")

        @p.on("scalar")
        def _(s, tt=tt, b=b):
            p.wait(s, "gev", p.tok("gev"))
            p.inc(s.activation(prtile[b][:], ps_g[:, 0:8], AF.Exp,
                               bias=nmx[b][:, 0:1], accum_out=ssum[b][:]), "sax")

        @p.on("vector")
        def _(v, tt=tt, b=b):
            p.wait(v, "sax", p.tok("sax"))
            p.inc(v.reciprocal(ssum[b][:], ssum[b][:]), "gev")
            p.wait(v, "gev")
            p.inc(v.tensor_scalar(prtile[b][:], prtile[b][:], ssum[b][:, 0:1], None,
                                  ALU.mult, ALU.bypass), "gev")
            p.wait(v, "gev")
            p.inc(v.tensor_copy(prbf[b][:], prtile[b][:]), "gev")

        @p.on("sync")
        def _(sync, tt=tt, b=b, r=r):
            p.wait(sync, "gev", p.tok("gev"))
            dma = sync.dma_start(out=xlnh2[r, 0:8], in_=prbf[b][:])
            p.inc(dma, f"gs{b}", 16)

        if stage >= 90 or stage < 90:
            pass
        if not globals().get("MOE_NO_TRANSPOSE", False):
            @p.on("tensor")
            def _(t, tt=tt, b=b):
                p.wait(t, "gev", p.tok("gev"))
                p.wait(t, "tev", p.tok("tev"))
                ins = t.transpose(ps_h[0:8, 0:128], prtile[b][:], idfs[:])
                p.inc(ins, "gmm")

            @p.on("vector")
            def _(v, tt=tt, r=r):
                p.wait(v, "gmm", p.tok("gmm"))
                p.inc(v.tensor_copy(prbt[0:8, r], ps_h[0:8, 0:128]), "tev")

    if stage < 2:
        p.finish()
        return p
    NITER = 40
    for it in range(NITER):
        @p.on("vector")
        def _(v, it=it):
            if it == 0:
                p.wait(v, "tev", p.tok("tev"))
                p.wait(v, "cast", ini_tok)
            else:
                p.wait(v, "bis")
            p.inc(v.tensor_scalar(junk8[:], prbt[:], th8[:, 0:1], None,
                                  ALU.is_ge, ALU.add, accum_out=cn8[:]), "bis")
            p.wait(v, "bis")
            p.inc(v.tensor_single_scalar(mm8[:], cn8[:], float(KSEL), ALU.is_ge), "bis")
            p.wait(v, "bis")
            p.inc(v.tensor_sub(dd8[:], th8[:], lo8[:]), "bis")
            p.wait(v, "bis")
            p.inc(v.tensor_mul(dd8[:], dd8[:], mm8[:]), "bis")
            p.wait(v, "bis")
            p.inc(v.tensor_add(lo8[:], lo8[:], dd8[:]), "bis")
            p.wait(v, "bis")
            p.inc(v.tensor_sub(dd8[:], hi8[:], th8[:]), "bis")
            p.wait(v, "bis")
            p.inc(v.tensor_mul(dd8[:], dd8[:], mm8[:]), "bis")
            p.wait(v, "bis")
            p.inc(v.tensor_add(hi8[:], th8[:], dd8[:]), "bis")
            p.wait(v, "bis")
            p.inc(v.tensor_add(th8[:], lo8[:], hi8[:]), "bis")
            p.wait(v, "bis")
            p.inc(v.tensor_scalar(th8[:], th8[:], 0.5, None, ALU.mult, ALU.bypass), "bis")

    @p.on("vector")
    def _(v):
        p.wait(v, "bis")
        p.inc(v.tensor_scalar(msk8[:], prbt[:], lo8[:, 0:1], None,
                              ALU.is_ge, ALU.bypass), "cmp")
        p.inc(v.memset(zrow8[:], 0.0), "cmp")
        p.wait(v, "cmp")
        p.inc(v.tensor_tensor_scan(dst8[:], msk8[:], zrow8[:], 0.0, ALU.add, ALU.add), "cmp")
        p.wait(v, "cmp")
        p.inc(v.tensor_scalar(dst8[:], dst8[:], bia8[:, 0:1], None, ALU.add, ALU.bypass), "cmp")
        p.wait(v, "cmp")
        def punish(v, keepm):
            p.inc(v.tensor_scalar(tmp8[:], keepm, -BIG, None, ALU.mult, ALU.bypass), "cmp")
            p.wait(v, "cmp")
            p.inc(v.tensor_scalar(tmp8[:], tmp8[:], BIG, None, ALU.add, ALU.bypass), "cmp")
            p.wait(v, "cmp")
            p.inc(v.tensor_add(dst8[:], dst8[:], tmp8[:]), "cmp")
            p.wait(v, "cmp")
        punish(v, msk8[:])
        p.inc(v.tensor_scalar(msk8[:], dst8[:], 0.0, None, ALU.is_ge, ALU.bypass), "cmp")
        p.wait(v, "cmp")
        punish(v, msk8[:])
        p.inc(v.tensor_scalar(msk8[:], dst8[:], float(CAP - 1), None, ALU.is_le, ALU.bypass), "cmp")
        p.wait(v, "cmp")
        punish(v, msk8[:])
    cmp_tok = p.tok("cmp")

    for tt in range(ntt):
        r = slice(tt * 128, (tt + 1) * 128)

        @p.on("tensor")
        def _(t, tt=tt, r=r):
            p.wait(t, "cmp", cmp_tok)
            p.wait(t, "tev", p.tok("tev"))
            ins = t.transpose(ps_h2[0:128, 0:8], dst8[0:8, r], idfs[0:8, 0:8])
            p.inc(ins, "gmm")

        @p.on("vector")
        def _(v, tt=tt):
            p.wait(v, "gmm", p.tok("gmm"))
            p.inc(v.tensor_copy(idxf[:, tt * 8:(tt + 1) * 8], ps_h2[0:128, 0:8]), "tev")
            p.wait(v, "tev")
            p.inc(v.tensor_copy(idxtm[:, tt * 8:(tt + 1) * 8], idxf[:, tt * 8:(tt + 1) * 8]), "tev")
    idx_tok = p.tok("tev")

    if stage < 3:
        p.finish()
        return p
    for tt in range(ntt):
        b = tt % 2

        @p.on("sync")
        def _(sync, tt=tt, b=b):
            if tt == 0:
                p.wait(sync, "ln_o0")
                p.wait(sync, "ln_o1")
                p.wait(sync, "gs0")
                p.wait(sync, "gs1")
            if tt >= 2:
                for e in range(NEXP):
                    p.wait(sync, f"se{e}", p.tok(f"se{e}"))
            dma = sync.dma_start(out=xgld[b][:], in_=xlnh2[tt * 128:(tt + 1) * 128, :])
            p.inc(dma, f"xg{b}", 16)

        for e in range(NEXP):
            @p.on("gpsimd")
            def _(g, tt=tt, b=b, e=e):
                if tt == 0 and e == 0:
                    p.wait(g, "tev", idx_tok)
                if e == 0:
                    p.wait(g, f"xg{b}", p.tok(f"xg{b}"))
                p.wait(g, f"se{e}", p.tok(f"se{e}"))
                ins = g.indirect_dma_start(
                    out=xse[e],
                    out_offset=IndirectOffsetOnAxis(ap=idxtm[:, tt * 8 + e: tt * 8 + e + 1], axis=0),
                    in_=xgld[b][:], in_offset=None,
                    bounds_check=breg, oob_is_err=False,
                )
                p.inc(ins, f"se{e}", 16)
    sct_toks = [p.tok(f"se{e}") for e in range(NEXP)]

    if stage < 4:
        p.finish()
        return p
    for e in range(NEXP):
        @p.on("scalar")
        def _(s, e=e):
            p.wait(s, f"se{e}", sct_toks[e])
            if e > 0:
                p.wait(s, "mm", p.tok("mm"))
            for c in range(NCHUNK):
                dma = s.dma_start_transpose(
                    out=xsfm[c][:], in_=xse[e][:, 10 + c * 128: 10 + (c + 1) * 128])
                p.inc(dma, "trx", 16)

        @p.on("gpsimd")
        def _(g, e=e):
            if e > 0:
                p.wait(g, "h2s", p.tok("h2s"))
            p.wait(g, f"se{e}", sct_toks[e])
            for st in range(NST):
                dma = g.dma_start(out=gcol[:, st:st + 1],
                                  in_=xse[e][st * 128:(st + 1) * 128, e:e + 1],
                                  allow_slow_non_contiguous=True)
                p.inc(dma, "gld", 16)

        @p.on("vector")
        def _(v, e=e):
            if e > 0:
                p.wait(v, "h2s", p.tok("h2s"))
            if e == 0:
                p.wait(v, "gmm", p.tok("gmm"))
            p.inc(v.memset(h2a[0][:, 0:NST * D // 2], 0.0), "aev")
            p.inc(v.memset(h2a[1][:, 0:NST * D // 2], 0.0), "aev")
        hz_tok = p.tok("aev")

        for fg in range(8):
            @p.on("gpsimd")
            def _(g, e=e, fg=fg):
                if e > 0 or fg > 0:
                    p.wait(g, "mm", p.tok("mm"))
                else:
                    p.wait(g, "gmm", p.tok("gmm"))
                dma = g.dma_start(out=W1g[:, 0:4096],
                                  in_=w1[e][:, fg * 512:(fg + 1) * 512].rearrange("(c p) n -> p c n", p=128))
                p.inc(dma, "wld", 16)
                dma = g.dma_start(out=W2g[:, 0:4 * 1024],
                                  in_=w2[e][fg * 512:(fg + 1) * 512, :].rearrange("(c p) n -> p c n", p=128))
                p.inc(dma, "wld", 16)
            wld_tok = p.tok("wld")

            for fs in range(4):
                @p.on("tensor")
                def _(t, e=e, fg=fg, fs=fs, wld_tok=wld_tok):
                    p.wait(t, "wld", wld_tok)
                    if e == 0 and fg == 0 and fs == 0:
                        p.wait(t, "trx", p.tok("trx"))
                        p.wait(t, "gmm", p.tok("gmm"))
                    if fg == 0 and fs == 0 and e > 0:
                        p.wait(t, "trx", p.tok("trx"))
                    p.wait(t, "hact", p.tok("hact"))
                    for c in range(NCHUNK):
                        ins = t.matmul(ps_h[:, 0:CAP] if CAP <= 512 else None,
                                       W1g[:, c * 512 + fs * 128: c * 512 + fs * 128 + 128],
                                       xsfm[c][:],
                                       start=(c == 0), stop=(c == NCHUNK - 1))
                    p.inc(ins, "mm")

                @p.on("scalar")
                def _(s, e=e, fg=fg, fs=fs):
                    p.wait(s, "mm", p.tok("mm"))
                    if fg > 0 or fs > 0 or e > 0:
                        p.wait(s, "mm", p.tok("mm"))
                    p.inc(s.activation(hbuf[:, fs * CAP:(fs + 1) * CAP],
                                       ps_h[:, 0:CAP], AF.Gelu), "hact")

            for st in range(NST):
                for dc2 in range(2):
                    @p.on("tensor")
                    def _(t, e=e, fg=fg, st=st, dc2=dc2):
                        p.wait(t, "hact", p.tok("hact"))
                        p.wait(t, "aev", p.tok("aev"))
                        for ff in range(4):
                            ins = t.matmul(ps_h2[:],
                                           hbuf[:, ff * CAP + st * 128: ff * CAP + st * 128 + 128],
                                           W2g[:, ff * 1024 + dc2 * 512: ff * 1024 + dc2 * 512 + 512],
                                           start=(ff == 0), stop=(ff == 3))
                        p.inc(ins, "mm")

                    @p.on("vector")
                    def _(v, e=e, st=st, dc2=dc2, hz_tok=hz_tok):
                        p.wait(v, "mm", p.tok("mm"))
                        p.wait(v, "aev", hz_tok)
                        gidx = st * D + dc2 * 512
                        HALF = NST * D // 2
                        hh = h2a[gidx // HALF]
                        sl = slice(gidx % HALF, gidx % HALF + 512)
                        p.inc(v.tensor_add(hh[:, sl], hh[:, sl], ps_h2[:]), "aev")

        for st in range(NST):
            hb2 = (e * NST + st) % 2

            @p.on("gpsimd")
            def _(g, e=e, st=st):
                p.wait(g, f"se{e}", sct_toks[e])
                p.wait(g, "tidc", p.tok("tidc"))
                dma = g.dma_start(out=tid2[:], in_=xse[e][st * 128:(st + 1) * 128, 8:10],
                                  allow_slow_non_contiguous=True)
                p.inc(dma, "gld", 16)

            @p.on("vector")
            def _(v, e=e, st=st, hb2=hb2):
                p.wait(v, "gld", p.tok("gld"))
                p.wait(v, "tidw", p.tok("tidw"))
                p.inc(v.tensor_scalar(tid2[:, 0:1], tid2[:, 0:1], 64.0, None,
                                      ALU.mult, ALU.bypass), "tidc")
                p.wait(v, "tidc")
                p.inc(v.tensor_add(tid2[:, 0:1], tid2[:, 0:1], tid2[:, 1:2]), "tidc")
                p.wait(v, "tidc")
                p.inc(v.tensor_copy(tidi[:], tid2[:, 0:1]), "tidc")
                p.wait(v, "aev", p.tok("aev"))
                p.wait(v, "h2s", p.tok("h2s"))
                HALF = NST * D // 2
                for dd in range(D // 512):
                    gidx = st * D + dd * 512
                    hh = h2a[gidx // HALF]
                    p.inc(v.tensor_scalar(h2wt2[hb2][:, dd * 512:(dd + 1) * 512],
                                          hh[:, gidx % HALF: gidx % HALF + 512],
                                          gcol[:, st:st + 1], None, ALU.mult, ALU.bypass), "aev")
                    p.wait(v, "aev")
            wt_tok = p.tok("aev")

            @p.on("gpsimd")
            def _(g, e=e, st=st, hb2=hb2, wt_tok=wt_tok):
                p.wait(g, "tidc", p.tok("tidc"))
                p.wait(g, "aev", wt_tok)
                if e == 0 and st == 0:
                    p.wait(g, "zf", p.tok("zf"))
                p.wait(g, "h2s", p.tok("h2s"))
                ins = g.indirect_dma_start(
                    out=pm, out_offset=IndirectOffsetOnAxis(ap=tidi[:], axis=0),
                    in_=h2wt2[hb2][:], in_offset=None,
                    bounds_check=tbreg, oob_is_err=False,
                    compute_op=ALU.add,
                )
                p.inc(ins, "h2s", 16)
                p.inc(g.nop(nofuse=True), "tidw")
    h2_all = p.tok("h2s")

    @p.on("sync")
    def _(sync):
        p.wait(sync, "h2s", h2_all)
    @p.on("sync")
    def _(sync):
        p.wait(sync, "ln_c0")
        p.wait(sync, "ln_c1")

    p.finish()
    return p


def host_moe_weights(inp, layer, j, T):
    CAP = T // 8
    t = np.arange(T)
    tokid = np.stack([(t // 64).astype(BF), (t % 64).astype(BF)], 1)
    return dict(
        tokid=np.ascontiguousarray(tokid),
        gatew=np.ascontiguousarray(inp[f"L{layer}_ffn_gate_w"]),
        w1=np.ascontiguousarray(inp[f"L{layer}_ffn_w1"]),
        w2=np.ascontiguousarray(inp[f"L{layer}_ffn_w2"]),
        slotb=np.full((8, 1), float(CAP * j), np.float32),
        ecap1=np.full((8, 1), -1.0, np.float32),
        idf32=np.eye(128, dtype=np.float32),
        cscal=np.tile(np.array([[EPS, 1.0, -6.0, -4.0]], np.float32), (128, 1)),
    )

def build_l0aa(T, mode="add"):
    p = Prog("l0aa")
    ntt = T // 128
    ntc = T // 512
    xprev = p.inp("xprev", [T, D])
    part1 = p.inp("part1", [T, D])
    part2 = p.inp("part2", [T, D])
    wQ = p.inp("Wq", [D, 512])
    wK = p.inp("Wk", [D, D])
    wV = p.inp("Wv", [D, D])
    wDn = p.inp("Wdn", [D, DL])
    wUk = p.inp("Wuk", [DL, 512])
    wUv = p.inp("Wuv", [DL, 512])
    wO = p.inp("Wo", [512, D])
    epp = p.inp("Epos_p", [128, T], BF16)
    epm = p.inp("Epos_m", [128, T], BF16)
    tmsk = p.inp("Tmsk", [128, 4 * 512], U8)
    cs = p.inp("cscal", [128, 4])
    xc_out = p.out("xc", [T, D])
    pout = p.out("pout", [T, D])
    qsp = p.dram("qsp", [2 * 512, T], BF16)
    oh = p.dram("oh", [512, T], BF16)

    xfm = [p.sb(f"xfm{c}", 128, T, BF16) for c in range(NCHUNK)]
    ka, kb, qa, qb = xfm[0], xfm[1], xfm[2], xfm[3]
    csb = p.sb("csb", 128, 4, F32)
    WQb = p.sb("WQb", 128, NCHUNK * 512, BF16)
    WKb = p.sb("WKb", 128, NCHUNK * D, BF16)
    WVb = p.sb("WVb", 128, NCHUNK * D, BF16)
    WDb = p.sb("WDb", 128, NCHUNK * DL, BF16)
    WUKb = p.sb("WUKb", 128, 2 * 512, BF16)
    WUVb = p.sb("WUVb", 128, 2 * 512, BF16)
    WOb = p.sb("WOb", 128, 4 * D, BF16)
    epps = p.sb("epps", 128, T, BF16)
    epms = p.sb("epms", 128, T, BF16)
    tmsks = p.sb("tmsks", 128, 4 * 512, U8)
    lat = [p.sb(f"lat{lc}", 128, T, BF16) for lc in range(2)]
    kvb = p.sb("kvb", 128, 512, BF16)
    vaug = p.sb("vaug", 128, ntt * 130, BF16)
    exps = [p.sb(f"exps{i}", 128, 512, BF16) for i in range(3)]
    dnm = p.sb("dnm", 1, 512, F32)
    bcs = p.sb("bcs", 64, 512, F32)
    onesw = p.sb("onesw", 1, 64, F32)
    otl = p.sb("otl", 128, 512, BF16)
    obuf = [[exps[i][:, c * 128:(c + 1) * 128] for c in range(4)] for i in range(2)]
    pot = [p.sb("pot0", 128, D, F32)] * 2

    ps_k = p.ps("ps_k", 128, 512)
    ps_v = p.ps("ps_v", 128, 512)
    ps_l = [p.ps(f"ps_l{lc}", 128, 512) for lc in range(2)]
    ps_sA = p.ps("ps_sA", 128, 512)
    ps_sB = p.ps("ps_sB", 128, 512)
    ps_oA = p.ps("ps_oA", 128, 512)
    ps_oB = p.ps("ps_oB", 128, 512)

    for sname in ("ld", "cast", "mm", "ev", "lmm", "lev", "qmm", "qev", "qs",
                  "kev", "vev", "sev", "sel", "clp", "pmm", "pev", "oq", "obl",
                  "po0", "po1", "o0", "o1"):
        p.sem(sname)
    p.begin()

    @p.on("sync")
    def _(sync):
        p.inc(sync.dma_start(out=csb[:], in_=cs), "ld", 16)
        p.inc(sync.dma_start(out=epps[:], in_=epp), "ld", 16)
        p.inc(sync.dma_start(out=epms[:], in_=epm), "ld", 16)
        p.inc(sync.dma_start(out=tmsks[:], in_=tmsk), "ld", 16)
    sp = StagePair(p, "stg", D)
    wq_tok = emit_weight_to(p, wQ, D, 512, WQb, "Wq", sp, "cast")
    wk_tok = emit_weight_to(p, wK, D, D, WKb, "Wk", sp, "cast")
    wv_tok = emit_weight_to(p, wV, D, D, WVb, "Wv", sp, "cast")
    wd_tok = emit_weight_to(p, wDn, D, DL, WDb, "Wdn", sp, "cast")
    wuk_tok = emit_weight_to(p, wUk, DL, 512, WUKb, "Wuk", sp, "cast")
    wuv_tok = emit_weight_to(p, wUv, DL, 512, WUVb, "Wuv", sp, "cast")
    wo_tok = emit_weight_to(p, wO, 512, D, WOb, "Wo", sp, "cast")
    ld0 = p.tok("ld")

    @p.on("vector")
    def _(v):
        p.wait(v, "ld", ld0)
        p.inc(v.memset(onesw[:], 1.0), "cast")
        p.inc(v.memset(vaug[:], 1.0), "cast")
    ini_tok = p.tok("cast")

    @p.on("scalar")
    def _(s):
        p.wait(s, "ld", ld0)
    emit_prologue(p, xprev, part1, part2, mode, T, xc_out, p.dram("xlnh", [T, D], BF16),
                  xfm, csb[:, 0:1], "ln")
    fm_tok = p.tok("ln_f")

    for hp in range(4):
        for tc in range(ntc):
            @p.on("tensor")
            def _(t, hp=hp, tc=tc):
                if hp == 0 and tc == 0:
                    p.wait(t, "ln_f", fm_tok)
                    p.wait(t, wq_tok)
                p.wait(t, "qev", p.tok("qev"))
                for k in range(NCHUNK):
                    ins = t.matmul(ps_k[:], WQb[:, k * 512 + hp * 128: k * 512 + hp * 128 + 128],
                                   xfm[k][:, tc * 512:(tc + 1) * 512],
                                   start=(k == 0), stop=(k == NCHUNK - 1))
                p.inc(ins, "qmm")

            @p.on("vector")
            def _(v, hp=hp, tc=tc):
                p.wait(v, "qmm", p.tok("qmm"))
                p.wait(v, "qs", p.tok("qs"))
                tr = slice(tc * 512, (tc + 1) * 512)
                p.inc(v.tensor_mul(kvb[:], ps_k[:], epms[:, tr]), "qev")
                p.wait(v, "qev")
                p.inc(v.tensor_mul(otl[:], ps_k[:], epps[:, tr]), "qev")

            @p.on("sync")
            def _(sync, hp=hp, tc=tc):
                p.wait(sync, "qev", p.tok("qev"))
                tr = slice(tc * 512, (tc + 1) * 512)
                dma = sync.dma_start(out=qsp[hp * 128:(hp + 1) * 128, tr], in_=kvb[:])
                p.inc(dma, "qs", 16)
                dma = sync.dma_start(out=qsp[512 + hp * 128: 512 + (hp + 1) * 128, tr], in_=otl[:])
                p.inc(dma, "qs", 16)
    qs_all = p.tok("qs")

    for tc in range(ntc):
        for dc in range(NCHUNK):
            @p.on("tensor")
            def _(t, tc=tc, dc=dc):
                if tc == 0 and dc == 0:
                    p.wait(t, wd_tok)
                    p.wait(t, wk_tok)
                    p.wait(t, wv_tok)
                    p.wait(t, "qs", qs_all)
                p.wait(t, "ev", p.tok("ev"))
                for k in range(NCHUNK):
                    t.matmul(ps_k[:], WKb[:, k * D + dc * 128: k * D + dc * 128 + 128],
                             xfm[k][:, tc * 512:(tc + 1) * 512],
                             start=(k == 0), stop=(k == NCHUNK - 1))
                for k in range(NCHUNK):
                    ins = t.matmul(ps_v[:], WVb[:, k * D + dc * 128: k * D + dc * 128 + 128],
                                   xfm[k][:, tc * 512:(tc + 1) * 512],
                                   start=(k == 0), stop=(k == NCHUNK - 1))
                p.inc(ins, "mm")

            @p.on("vector")
            def _(v, tc=tc, dc=dc):
                p.wait(v, "mm", p.tok("mm"))
                p.wait(v, "lmm", p.tok("lmm"))
                p.inc(v.tensor_copy(kvb[:], ps_k[:]), "ev")
                p.wait(v, "ev")
                p.inc(v.tensor_add(kvb[:], kvb[:], ps_v[:]), "ev")

            @p.on("tensor")
            def _(t, tc=tc, dc=dc):
                p.wait(t, "ev", p.tok("ev"))
                for lc in range(2):
                    ins = t.matmul(ps_l[lc][:], WDb[:, dc * DL + lc * 128: dc * DL + lc * 128 + 128],
                                   kvb[:], start=(dc == 0), stop=(dc == NCHUNK - 1))
                p.inc(ins, "lmm")

        @p.on("vector")
        def _(v, tc=tc):
            p.wait(v, "lmm", p.tok("lmm"))
            for lc in range(2):
                p.inc(v.tensor_copy(lat[lc][:, tc * 512:(tc + 1) * 512], ps_l[lc][:]), "lev")
    lat_tok = p.tok("lev")


    for hp in range(4):
        @p.on("sync")
        def _(sync, hp=hp):
            p.wait(sync, "lev", lat_tok)
            if hp == 0:
                p.wait(sync, "cast", ini_tok)
            p.wait(sync, "pmm", p.tok("pmm"))
            dma = sync.dma_start(out=qa[:], in_=qsp[hp * 128:(hp + 1) * 128, :])
            p.inc(dma, "oq", 16)
            dma = sync.dma_start(out=qb[:], in_=qsp[512 + hp * 128:512 + (hp + 1) * 128, :])
            p.inc(dma, "oq", 16)
        oq_tok = p.tok("oq")

        for tc in range(ntc):
            @p.on("tensor")
            def _(t, hp=hp, tc=tc):
                if hp == 0 and tc == 0:
                    p.wait(t, wuk_tok)
                p.wait(t, "oq", oq_tok)
                p.wait(t, "kev", p.tok("kev"))
                for lc in range(2):
                    ins = t.matmul(ps_k[:], WUKb[:, lc * 512 + hp * 128: lc * 512 + hp * 128 + 128],
                                   lat[lc][:, tc * 512:(tc + 1) * 512],
                                   start=(lc == 0), stop=(lc == 1))
                p.inc(ins, "mm")

            @p.on("vector")
            def _(v, hp=hp, tc=tc):
                p.wait(v, "mm", p.tok("mm"))
                tr = slice(tc * 512, (tc + 1) * 512)
                p.inc(v.tensor_mul(ka[:, tr], ps_k[:], epps[:, tr]), "kev")
                p.wait(v, "kev")
                p.inc(v.tensor_mul(kb[:, tr], ps_k[:], epms[:, tr]), "kev")

        for tt in range(ntt):
            @p.on("tensor")
            def _(t, hp=hp, tt=tt):
                if hp == 0 and tt == 0:
                    p.wait(t, wuv_tok)
                p.wait(t, "vev", p.tok("vev"))
                for lc in range(2):
                    ins = t.matmul(ps_v[:, 0:128], lat[lc][:, tt * 128:(tt + 1) * 128],
                                   WUVb[:, lc * 512 + hp * 128: lc * 512 + hp * 128 + 128],
                                   start=(lc == 0), stop=(lc == 1))
                p.inc(ins, "mm")

            @p.on("vector")
            def _(v, hp=hp, tt=tt):
                p.wait(v, "mm", p.tok("mm"))
                p.inc(v.tensor_copy(vaug[:, tt * 130: tt * 130 + 64], ps_v[:, 0:64]), "vev")
                p.inc(v.tensor_copy(vaug[:, tt * 130 + 65: tt * 130 + 129], ps_v[:, 64:128]), "vev")
        kv_tok = (p.tok("kev"), p.tok("vev"))

        for qc in range(ntc):
            for h2 in range(2):
                @p.on("tensor")
                def _(t, h2=h2, kv_tok=kv_tok):
                    p.wait(t, "kev", kv_tok[0])
                    p.wait(t, "vev", kv_tok[1])
                    p.wait(t, "pev", p.tok("pev"))
                    p.inc(t.nop(nofuse=True), "pmm")
            ps_o2 = (ps_oA, ps_oB)
            first_pv = [True, True]
            for kt in range(ntt):
                lo_q = qc * 512
                hi_q = lo_q + 511
                lo_k = kt * 128
                hi_k = lo_k + 127
                if hi_k < lo_q:
                    modes = [("A",)]
                elif lo_k > hi_q:
                    modes = [("B",)]
                else:
                    modes = [("M", (kt - 4 * qc))]
                for h2 in range(2):
                    hb = h2 * 64
                    m = modes[0]

                    @p.on("tensor")
                    def _(t, qc=qc, kt=kt, h2=h2, hb=hb, m=m):
                        p.wait(t, "sev", p.tok("sev"))
                        if m[0] in ("A", "M"):
                            ins = t.matmul(ps_sA[:], ka[hb:hb + 64, kt * 128:(kt + 1) * 128],
                                           qa[hb:hb + 64, qc * 512:(qc + 1) * 512],
                                           start=True, stop=True)
                        if m[0] in ("B", "M"):
                            ins = t.matmul(ps_sB[:], kb[hb:hb + 64, kt * 128:(kt + 1) * 128],
                                           qb[hb:hb + 64, qc * 512:(qc + 1) * 512],
                                           start=True, stop=True)
                        p.inc(ins, "mm")

                    if m[0] == "M":
                        @p.on("vector")
                        def _(v):
                            p.wait(v, "mm", p.tok("mm"))
                            p.inc(v.tensor_scalar_min(ps_sB[:], ps_sB[:], 2560.0), "clp")
                            p.inc(v.tensor_scalar_min(ps_sA[:], ps_sA[:], 2560.0), "clp")

                    @p.on("scalar")
                    def _(s, m=m):
                        p.wait(s, "mm", p.tok("mm"))
                        p.wait(s, "clp", p.tok("clp"))
                        p.wait(s, "sel", p.tok("sel"))
                        if m[0] in ("A", "M"):
                            p.inc(s.activation(exps[0][:], ps_sA[:], AF.Exp,
                                               scale=1.0 / 32.0, bias=csb[:, 3:4]), "sev")
                        if m[0] in ("B", "M"):
                            p.inc(s.activation(exps[1][:], ps_sB[:], AF.Exp,
                                               scale=1.0 / 32.0, bias=csb[:, 3:4]), "sev")

                    @p.on("vector")
                    def _(v, m=m):
                        p.wait(v, "sev", p.tok("sev"))
                        if m[0] == "M":
                            r = m[1]
                            p.inc(v.select(exps[2][:], tmsks[:, r * 512:(r + 1) * 512],
                                           exps[0][:], exps[1][:], add_drain=True), "sel")
                        else:
                            p.inc(v.nop(nofuse=True), "sel")
                    src = {"A": 0, "B": 1, "M": 2}[m[0]]

                    @p.on("tensor")
                    def _(t, qc=qc, kt=kt, h2=h2, src=src, fp=tuple(first_pv)):
                        p.wait(t, "sel", p.tok("sel"))
                        ins = t.matmul(ps_o2[h2][0:65, :],
                                       vaug[:, kt * 130 + 65 * h2: kt * 130 + 65 * h2 + 65],
                                       exps[src][:],
                                       start=fp[h2], stop=(kt == ntt - 1))
                        p.inc(ins, "pmm")
                    first_pv[h2] = False

            for h2 in range(2):
                @p.on("vector")
                def _(v, h2=h2):
                    p.wait(v, "pmm", p.tok("pmm"))
                    p.inc(v.tensor_copy(dnm[:], ps_o2[h2][64:65, :]), "pev")
                    p.wait(v, "pev")
                    p.inc(v.reciprocal(dnm[:], dnm[:]), "pev")

                @p.on("tensor")
                def _(t, h2=h2):
                    p.wait(t, "pev", p.tok("pev"))
                    ins = t.matmul(ps_v[0:64, :], onesw[:], dnm[:], start=True, stop=True)
                    p.inc(ins, "pmm")

                @p.on("vector")
                def _(v, h2=h2, hp=hp, qc=qc):
                    p.wait(v, "pmm", p.tok("pmm"))
                    p.inc(v.tensor_copy(bcs[:], ps_v[0:64, :]), "pev")
                    p.wait(v, "pev")
                    p.wait(v, "obl", p.tok("obl"))
                    p.inc(v.tensor_mul(otl[h2 * 64:(h2 + 1) * 64, :],
                                       ps_o2[h2][0:64, :], bcs[:]), "pev")

            @p.on("sync")
            def _(sync, hp=hp, qc=qc):
                p.wait(sync, "pev", p.tok("pev"))
                dma = sync.dma_start(out=oh[hp * 128:(hp + 1) * 128, qc * 512:(qc + 1) * 512],
                                     in_=otl[:]).then_inc(p.sems["obl"], 16)
                p.cnt["obl"] += 16
    ob_all = p.tok("obl")

    for tt in range(ntt):
        ob = tt % 2

        @p.on("scalar")
        def _(s, tt=tt, ob=ob):
            if tt == 0:
                p.wait(s, "obl", ob_all)
            p.wait(s, f"po{ob}", p.tok(f"po{ob}"))
            p.wait(s, "mm", p.tok("mm"))
            for c in range(4):
                dma = s.dma_start(out=obuf[ob][c],
                                  in_=oh[c * 128:(c + 1) * 128, tt * 128:(tt + 1) * 128])
                p.inc(dma, f"po{ob}", 16)
        po_tok = p.tok(f"po{ob}")

        for dc2 in range(2):
            @p.on("tensor")
            def _(t, tt=tt, ob=ob, dc2=dc2, po_tok=po_tok):
                if tt == 0 and dc2 == 0:
                    p.wait(t, wo_tok)
                p.wait(t, f"po{ob}", po_tok)
                p.wait(t, "ev", p.tok("ev"))
                for c in range(4):
                    ins = t.matmul(ps_k[:], obuf[ob][c],
                                   WOb[:, c * D + dc2 * 512: c * D + dc2 * 512 + 512],
                                   start=(c == 0), stop=(c == 3))
                p.inc(ins, "mm")

            @p.on("vector")
            def _(v, ob=ob, dc2=dc2):
                p.wait(v, "mm", p.tok("mm"))
                p.wait(v, "o0", p.tok("o0"))
                p.wait(v, "o1", p.tok("o1"))
                p.inc(v.tensor_copy(pot[ob][:, dc2 * 512:(dc2 + 1) * 512], ps_k[:]), "ev")

        @p.on("sync")
        def _(sync, tt=tt, ob=ob):
            p.wait(sync, "ev", p.tok("ev"))
            dma = sync.dma_start(out=pout[tt * 128:(tt + 1) * 128, :], in_=pot[ob][:])
            p.inc(dma, f"o{ob}", 16)

    @p.on("sync")
    def _(sync):
        p.wait(sync, "o0")
        p.wait(sync, "o1")
        p.wait(sync, "ln_c0")
        p.wait(sync, "ln_c1")

    p.finish()
    return p


def host_l0aa_weights(inp, j, T):
    lam = LAM
    t = np.arange(T, dtype=np.float64)
    epp = np.tile(np.exp(lam * t).astype(BF)[None, :], (128, 1))
    epm = np.tile(np.exp(-lam * t).astype(BF)[None, :], (128, 1))
    pidx = np.arange(128)[:, None]
    jidx = np.arange(512)[None, :]
    tm = np.concatenate([(pidx + 128 * r <= jidx).astype(np.uint8) for r in range(4)], 1)
    return dict(
        Wq=np.ascontiguousarray(inp["L0_attn_a_W_q"][:, 512 * j:512 * j + 512]),
        Wk=np.ascontiguousarray(inp["L0_attn_a_W_k"]),
        Wv=np.ascontiguousarray(inp["L0_attn_a_W_v"]),
        Wdn=np.ascontiguousarray(inp["L0_attn_a_W_down_kv"]),
        Wuk=np.ascontiguousarray(inp["L0_attn_a_W_up_k"][:, 512 * j:512 * j + 512]),
        Wuv=np.ascontiguousarray(inp["L0_attn_a_W_up_v"][:, 512 * j:512 * j + 512]),
        Wo=np.ascontiguousarray(inp["L0_attn_a_W_o"][512 * j:512 * j + 512, :]),
        Epos_p=np.ascontiguousarray(epp),
        Epos_m=np.ascontiguousarray(epm),
        Tmsk=np.ascontiguousarray(tm),
        cscal=np.tile(np.array([[EPS, 1.0, -6.0, -4.0]], np.float32), (128, 1)),
    )

TFULL = 4096
_PROG_CACHE = {}


def _prog(name):
    if name not in _PROG_CACHE:
        builders = dict(
            l0aa=lambda: build_l0aa(TFULL, mode="add"),
            moe=lambda: build_moe(TFULL, mode="add"),
            moer=lambda: build_moe(TFULL, mode="add"),
            l0ab=lambda: build_l0ab(TFULL, mode="round"),
            l1aa=lambda: build_l1aa(TFULL, mode="round"),
            l1ffn=lambda: build_l1ffn(TFULL, mode="add"),
            l1ab=lambda: build_l1ab(TFULL, mode="add"),
        )
        _PROG_CACHE[name] = builders[name]()
    return _PROG_CACHE[name]


def _launch(progname, in_maps):
    p = _prog(progname)
    res = run_bass_kernel_spmd(p.nc, in_maps, core_ids=list(range(NCORES)))
    return res.results


def kernel(**inputs):
    Z = np.zeros((TFULL, D), np.float32)
    cs = np.tile(np.array([[EPS, 1.0, -6.0, -4.0]], np.float32), (128, 1))
    x0 = [np.ascontiguousarray(np.asarray(inputs["L0_attn_a_x"][b], np.float32))
          for b in range(NB)]

    w_aa = [host_l0aa_weights(inputs, j, TFULL) for j in range(2)]
    r = _launch("l0aa", [dict(xprev=x0[c // 2], part1=Z, part2=Z, **w_aa[c % 2])
                         for c in range(NCORES)])
    pA = [np.asarray(r[c]["pout"], np.float32) for c in range(NCORES)]

    w_m0 = [host_moe_weights(inputs, 0, j, TFULL) for j in range(2)]
    r = _launch("moe", [dict(xprev=x0[c // 2], part1=pA[(c // 2) * 2], part2=pA[(c // 2) * 2 + 1],
                             **w_m0[c % 2]) for c in range(NCORES)])
    x1 = [np.asarray(r[2 * b]["xc"], np.float32) for b in range(NB)]
    pM = [np.asarray(r[c]["pm"], np.float32) for c in range(NCORES)]

    w_ab = host_l0ab_weights(inputs)
    r = _launch("l0ab", [dict(xprev=x1[c // 2], part1=pM[(c // 2) * 2], part2=pM[(c // 2) * 2 + 1],
                              **w_ab) for c in range(NCORES)])
    x3 = [np.asarray(r[2 * b]["xn"], np.float32) for b in range(NB)]

    r = _launch("moe", [dict(xprev=x3[c // 2], part1=Z, part2=Z, **w_m0[c % 2])
                        for c in range(NCORES)])
    pM2 = [np.asarray(r[c]["pm"], np.float32) for c in range(NCORES)]

    cons = host_l1aa_consts(TFULL)
    w_l1 = [dict(host_l1aa_weights(inputs, j), **cons, cscal=cs) for j in range(2)]
    r = _launch("l1aa", [dict(xprev=x3[c // 2], part1=pM2[(c // 2) * 2], part2=pM2[(c // 2) * 2 + 1],
                              **w_l1[c % 2]) for c in range(NCORES)])
    x4 = [np.asarray(r[2 * b]["xc"], np.float32) for b in range(NB)]
    pC = [np.asarray(r[c]["pout"], np.float32) for c in range(NCORES)]

    w_ff = host_l1ffn_weights(inputs)
    r = _launch("l1ffn", [dict(xprev=x4[c // 2], part1=pC[(c // 2) * 2], part2=pC[(c // 2) * 2 + 1],
                               **w_ff) for c in range(NCORES)])
    x6 = [np.asarray(r[2 * b]["xn"], np.float32) for b in range(NB)]

    w_hb = host_l1ab_weights(inputs)
    r = _launch("l1ab", [dict(xprev=x6[c // 2], part1=Z, part2=Z, **w_hb)
                         for c in range(NCORES)])
    x7 = [np.asarray(r[2 * b]["xn"], np.float32) for b in range(NB)]

    r = _launch("l1ffn", [dict(xprev=x7[c // 2], part1=Z, part2=Z, **w_ff)
                          for c in range(NCORES)])
    x8 = [np.asarray(r[2 * b]["xn"], np.float32) for b in range(NB)]

    return np.stack(x8, 0)
